# version 01
# baseline (speedup 1.0000x reference)
import os
import numpy as np
import concourse.bass as bass
import concourse.mybir as mybir
import concourse.bass_utils as bass_utils

F32 = mybir.dt.float32
AF = mybir.ActivationFunctionType
ALU = mybir.AluOpType
AX = mybir.AxisListType

NC = 8
L = 2048
DM = 1024
DI = 2048
N = 16
T = 256
NCH = L // T
NDT = DI // 128
TS = 128
HB = 256
NHB = 4096 // HB

ENGS = ("pe", "act", "dve", "pool", "sp")


def r3(dram_ap, p=128):
    return dram_ap.rearrange("(a p) c -> p a c", p=p)


def s3(sbuf_ap, a):
    return sbuf_ap.rearrange("p (a c) -> p a c", a=a)


NDMASEM = 12


class KB:
    def __init__(self, nc):
        self.nc = nc
        self.q = {e: [] for e in ENGS}
        self.semnames = (["pe", "act", "dve", "pool"] +
                         [f"spq{i}" for i in range(NDMASEM)] +
                         [f"poolq{i}" for i in range(NDMASEM)])
        self.cnt = {s: 0 for s in self.semnames}
        self.h = {}
        self.lastwait = {e: {} for e in ENGS}
        self.bank_tok = {}
        self.free = {}
        self.scan_guard = {}
        self.rr = {"spq": 0, "poolq": 0}

    def _wait(self, eng, tok):
        if tok is None:
            return
        sem, val = tok
        lw = self.lastwait[eng].get(sem, 0)
        if val <= lw:
            return
        self.lastwait[eng][sem] = val

        def f(p, s=sem, v=val):
            p.wait_ge(self.h[s], v)
        self.q[eng].append(f)

    def op(self, eng, fn, deps=(), tok=False):
        for d in deps:
            self._wait(eng, d)
        if tok:
            self.cnt[eng] += 1

            def f(p, fn=fn, e=eng):
                fn(p).then_inc(self.h[e], 1)
            self.q[eng].append(f)
            return (eng, self.cnt[eng])

        def f(p, fn=fn):
            fn(p)
        self.q[eng].append(f)
        return None

    def dma(self, qn, out, in_, deps=(), slow=False):
        eng = "sp" if qn == "spq" else "pool"
        sem = f"{qn}{self.rr[qn]}"
        self.rr[qn] = (self.rr[qn] + 1) % NDMASEM
        self._wait(eng, (sem, self.cnt[sem]))
        for d in deps:
            self._wait(eng, d)
        self.cnt[sem] += 16

        def f(p, o=out, i=in_, s=sem, slow=slow):
            if slow:
                p.dma_start(out=o, in_=i, allow_slow_non_contiguous=True).then_inc(self.h[s], 16)
            else:
                p.dma_start(out=o, in_=i).then_inc(self.h[s], 16)
        self.q[eng].append(f)
        return (sem, self.cnt[sem])

    def mm_group(self, bank, mms, evac_eng, evac_fn, extra_deps=()):
        dep = self.bank_tok.get(bank)
        deps0 = [d for d in extra_deps if d is not None] + ([dep] if dep else [])
        for i, m in enumerate(mms[:-1]):
            self.op("pe", m, deps=deps0 if i == 0 else ())
        pet = self.op("pe", mms[-1], deps=deps0 if len(mms) == 1 else (), tok=True)
        ct = self.op(evac_eng, evac_fn, deps=[pet], tok=True)
        self.bank_tok[bank] = ct
        return ct


def build_program():
    nc = bass.Bass("TRN2", target_bir_lowering=False, debug=False)
    k = KB(nc)

    def din(name, shape):
        return nc.dram_tensor(name, list(shape), F32, kind="ExternalInput").ap()

    x_in = din("x", (L + 3, DM))
    w_in_t = [din("w0_in_t", (DM, 2 * DI)), din("w1_in", (DM, 2 * DI))]
    w_conv = [din("w0_conv", (DI, 4)), din("w1_conv", (DI, 4))]
    w_convb = [din("w0_convb", (DI, 1)), din("w1_convb", (DI, 1))]
    w_x_t = [din("w0_x_t", (DI, 96)), din("w1_x", (DI, 33))]
    w0_dt_t = din("w0_dt_t", (64, DI))
    w_dtb = [din("w0_dtb", (DI, 1)), din("w1_dtb", (DI, 1))]
    w_alog = [din("w0_alog", (DI, N)), din("w1_alog", (DI, N))]
    w_dp = [din("w0_dp", (DI, 1)), din("w1_dp", (DI, 1))]
    w_out = [din("w0_out", (DI, DM)), din("w1_out", (DI, DM))]
    w1_wdt = din("w1_wdt", (DI, 1))
    w1_normw = din("w1_normw", (1, DM))
    f0_wr = din("f0_wr", (DM, DM))
    f0_wi = din("f0_wi", (DM, DM))
    f0_b = din("f0_b", (1, DM))
    m_gw_t = din("m_gw_t", (DM, 2))
    m_w1 = [din("m_w1e0", (DM, 4096)), din("m_w1e1", (DM, 4096)), din("m_sw1", (DM, 4096))]
    m_b1 = [din("m_b1e0", (1, 4096)), din("m_b1e1", (1, 4096)), din("m_sb1", (1, 4096))]
    m_w2 = [din("m_w2e0", (4096, DM)), din("m_w2e1", (4096, DM)), din("m_sw2", (4096, DM))]
    m_b2e0 = din("m_b2e0", (1, DM))
    m_b2e1 = din("m_b2e1", (1, DM))
    m_sb2 = din("m_sb2", (1, DM))
    c_eye = din("c_eye", (128, 128))
    c_grd = din("c_grd", (DM, DM))
    c_gnid = din("c_gnid", (DM, DM))
    c_frs = din("c_frs", (DM, 128))
    c_fips = din("c_fips", (DM, 128))
    c_fins = din("c_fins", (DM, 128))
    c_chm = din("c_chm", (128, 8))
    c_cst = din("c_cst", (128, 8))
    c_ones = din("c_ones", (1, 128))
    c_zeros = din("c_zeros", (128, 256))
    c_sel3 = din("c_sel3", (24, 3))

    out_t = nc.dram_tensor("out", [L, DM], F32, kind="ExternalOutput").ap()

    def dsc(name, shape):
        return nc.dram_tensor(name, list(shape), F32, kind="Internal")

    xr = dsc("xr", (L, DM))
    sp_u = dsc("sp_u", (DI, L))
    sp_d = dsc("sp_d", (DI, L))
    sp_z = dsc("sp_z", (DI, L))
    sp_bc = [dsc("sp_bc0", (33, L)), dsc("sp_bc1", (33, L))]
    sp_xnt = dsc("sp_xnt", (DM, L))
    sp_m0 = dsc("sp_m0", (1, L))
    sp_k = dsc("sp_k", (1, 2 * DM))
    ag0_in = dsc("ag0_in", (128, DM))
    ag0_out = dsc("ag0_out", (DM, DM))
    ag_in = [dsc("ag1_in", (128, 512)), dsc("ag3_in", (128, 512))]
    ag_out = [dsc("ag1_out", (1024, 512)), dsc("ag3_out", (1024, 512))]
    ag2_in = dsc("ag2_in", (3, DM))
    ag2_out = dsc("ag2_out", (24, DM))

    POOLC = 47104
    PB = nc.alloc_sbuf_tensor("PB", [128, POOLC], F32)

    def pv(off, ncols):
        assert off % 8 == 0 and off + ncols <= POOLC, (off, ncols)
        return PB[:, off:off + ncols]

    O = {}
    _o = [0]

    def seg(name, n):
        O[name] = _o[0]
        _o[0] += n
        return pv(O[name], n)

    xwin = seg("xwin", 2048)
    uT = seg("uT", 4096)
    dT = seg("dT", 4096)
    zT = seg("zT", 4096)
    Btl = seg("Btl", 4096)
    Ctl = seg("Ctl", 4096)
    abuf = [seg("a0", 2048), seg("a1", 2048)]
    bbuf = [seg("b0", 2048), seg("b1", 2048)]
    hbuf = [seg("h0", 2048), seg("h1", 2048)]
    PLC = seg("PLC", 8192)
    wb = [seg("wb0", 2048), seg("wb1", 2048)]
    assert _o[0] <= POOLC, _o[0]

    xn_v = PLC[:, 0:1024]
    xnT_v = PLC[:, 1024:3072]
    xpT_v = PLC[:, 3072:3072 + 16 * (T + 3)]
    du_v = PLC[:, 0:4096]
    ygT_v = PLC[:, 4096:8192]
    agsb = PLC[:, 4096:8192]
    F_PT = uT
    F_QT = dT
    F_FR = zT[:, 0:1024]
    F_FI2 = Btl[:, 0:2048]
    F_WS = [Ctl[:, 0:2048], Ctl[:, 2048:4096]]
    F_GM = [hbuf[0][:, 0:512], hbuf[0][:, 512:1024], hbuf[1][:, 0:512], hbuf[1][:, 512:1024]]
    F_M = PLC[:, 0:1024]
    Mres = [Btl, Ctl]
    MO = {}
    _m = [0]

    def mseg(name, n):
        MO[name] = _m[0]
        _m[0] += n
        return pv(MO[name], n)

    y_acc = mseg("y_acc", 16384)
    hbv = [mseg("hb0", 1024), mseg("hb1", 1024), mseg("hbs", 1024)]
    w1s = [mseg("w1s0", 2048), mseg("w1s1", 2048), mseg("w1s2", 2048)]
    w2s = [mseg("w2s0", 2048), mseg("w2s1", 2048), mseg("w2s2", 2048)]
    xnw = mseg("xnw", 4096)
    m0til = mseg("m0til", 2048)
    xw2 = mseg("xw2", 2048)
    xn2 = mseg("xn2", 1024)
    xnT2 = mseg("xnT2", 2048)
    ksb_v = mseg("ksb", 1024)
    b1r_v = mseg("b1r", 1024)
    assert _m[0] <= POOLC, _m[0]

    eye = nc.alloc_sbuf_tensor("eye", [128, 128], F32)
    A_sb = [nc.alloc_sbuf_tensor("A0", [128, 256], F32),
            nc.alloc_sbuf_tensor("A1", [128, 256], F32)]
    cc = [nc.alloc_sbuf_tensor("cc0", [128, 128], F32),
          nc.alloc_sbuf_tensor("cc1", [128, 128], F32)]
    lncol = nc.alloc_sbuf_tensor("lncol", [128, 16], F32)
    xdbl = nc.alloc_sbuf_tensor("xdbl", [128, 256], F32)
    Stot = nc.alloc_sbuf_tensor("Stot", [128, 16], F32)
    Dtot = nc.alloc_sbuf_tensor("Dtot", [128, 256], F32)
    carry = nc.alloc_sbuf_tensor("carry", [128, 256], F32)
    hin = nc.alloc_sbuf_tensor("hin", [128, 256], F32)
    chm = nc.alloc_sbuf_tensor("chm", [128, 8], F32)
    onesr = nc.alloc_sbuf_tensor("onesr", [1, 128], F32)
    m0buf = nc.alloc_sbuf_tensor("m0buf", [128, 16], F32)
    ycol = nc.alloc_sbuf_tensor("ycol", [128, 256], F32)
    tcol = nc.alloc_sbuf_tensor("tcol", [128, 256], F32)
    cst = nc.alloc_sbuf_tensor("cst", [128, 8], F32)
    dtrawb = ycol[:, 0:256]
    gcol = xdbl[:, 0:256]
    hxs = nc.alloc_sbuf_tensor("hxs", [128, 1024], F32)
    sel3 = nc.alloc_sbuf_tensor("sel3", [24, 3], F32)
    xnTh = nc.alloc_sbuf_tensor("xnTh", [128, 24], F32)
    kbuf = nc.alloc_sbuf_tensor("kbuf", [128, 96], F32)
    b1col = nc.alloc_sbuf_tensor("b1col", [128, 96], F32)
    kcols = nc.alloc_sbuf_tensor("kcols", [128, 16], F32)
    gwsb = nc.alloc_sbuf_tensor("gwsb", [128, 16], F32)
    normw = nc.alloc_sbuf_tensor("normw", [128, 1024], F32)
    f0b = nc.alloc_sbuf_tensor("f0b", [1, 1024], F32)
    ksb = PB[0:1, MO["xw2"]:MO["xw2"] + 2048]
    crow = PB[0:1, MO["xnT2"]:MO["xnT2"] + 2048]
    sb2row = PB[0:1, MO["xn2"]:MO["xn2"] + 1024]

    ps = [nc.alloc_psum_tensor(f"ps{i}", [128, 512], F32) for i in range(8)]

    SPQ, PLQ = "spq", "poolq"
    RG = [list(range(NC))]

    st = {
        "wb_tok": [None, None],
        "xp_evac": [None] * NDT,
        "xp_tail": [None] * NDT,
        "btl_free": None, "ctl_free": None,
        "xr_tok": [None] * NCH,
        "u_sp": [[None] * NCH, [None] * NCH],
        "d_sp": [[None] * NCH, [None] * NCH],
        "z_sp": [[None] * NCH, [None] * NCH],
        "yg_tok": [None] * NDT,
        "carry_tok": [None] * NDT,
        "wo_tws": [None] * NDT,
        "xw_free": None, "uT_free": None, "dT_free": None, "zT_free": None,
        "xdbl_free": None, "tspd": None, "du_free": None, "bc_sp": [],
        "stot_last": None,
        "ygT_free": None, "gcol_free": None, "ag2src": None,
        "xn_free": None, "xnT_free": None,
    }

    t_eye = k.dma(SPQ, eye[:], c_eye)
    t_chm = k.dma(SPQ, chm[:], c_chm)
    t_sel3 = k.dma(SPQ, sel3[:], c_sel3)
    t_A = [None, None]
    t_cc = [None, None]
    for lay in range(2):
        csb = cc[lay]
        k.dma(SPQ, s3(csb[:, 0:64], 16), r3(w_conv[lay]))
        k.dma(SPQ, s3(csb[:, 64:80], 16), r3(w_convb[lay]), slow=True)
        k.dma(SPQ, s3(csb[:, 80:96], 16), r3(w_dp[lay]), slow=True)
        t_cc[lay] = k.dma(SPQ, s3(csb[:, 96:112], 16), r3(w_dtb[lay]), slow=True)
    t_wdt1 = k.dma(SPQ, s3(cc[1][:, 112:128], 16), r3(w1_wdt), slow=True)
    t_ones = k.dma(SPQ, onesr[:], c_ones)
    t_cst = k.dma(SPQ, cst[:], c_cst)
    for lay in range(2):
        tl = k.dma(SPQ, s3(A_sb[lay][:], 16), r3(w_alog[lay]))
        ta = k.op("act", lambda p, s=A_sb[lay]: p.activation(out=s[:], in_=s[:], func=AF.Exp),
                  deps=[tl], tok=True)
        t_A[lay] = k.op("dve", lambda p, s=A_sb[lay]: p.tensor_scalar_mul(s[:], s[:], cst[:, 5:6]),
                        deps=[ta, t_cst], tok=True)

    def emit_fold():
        tFR = k.dma(SPQ, s3(F_FR, 8), r3(c_frs))
        tFI = [k.dma(SPQ, s3(F_FI2[:, 0:1024], 8), r3(c_fins)),
               k.dma(SPQ, s3(F_FI2[:, 1024:2048], 8), r3(c_fips))]
        prev_grp = [None, None]
        for which, dstv in enumerate((F_PT, F_QT)):
            stat1 = f0_wr if which == 0 else f0_wi
            stat2 = f0_wi if which == 0 else f0_wr
            mv2 = F_FI2[:, 0:1024] if which == 0 else F_FI2[:, 1024:2048]
            for ot in range(8):
                par = ot % 2
                tw1 = k.dma(SPQ, s3(F_WS[par][:, 0:1024], 8),
                            r3(stat1[:, ot * 128:(ot + 1) * 128]),
                            deps=[prev_grp[par]] if prev_grp[par] else [])
                tw2 = k.dma(SPQ, s3(F_WS[par][:, 1024:2048], 8),
                            r3(stat2[:, ot * 128:(ot + 1) * 128]))
                bank = 4 + par
                mms = []
                for kt in range(8):
                    mms.append(lambda p, kt=kt, par=par, bank=bank:
                               p.matmul(ps[bank][:, 0:128], F_WS[par][:, kt * 128:(kt + 1) * 128],
                                        F_FR[:, kt * 128:(kt + 1) * 128], start=(kt == 0), stop=False))
                for kt in range(8):
                    mms.append(lambda p, kt=kt, par=par, bank=bank, mv2=mv2:
                               p.matmul(ps[bank][:, 0:128], F_WS[par][:, 1024 + kt * 128:1024 + (kt + 1) * 128],
                                        mv2[:, kt * 128:(kt + 1) * 128], start=False, stop=(kt == 7)))

                def evac(p, ot=ot, dstv=dstv, bank=bank):
                    return p.copy(dstv[:, ot * 128:(ot + 1) * 128], ps[bank][:, 0:128])
                prev_grp[par] = k.mm_group(f"b{bank}", mms, "act", evac,
                                           extra_deps=[tw1, tw2, tFR, tFI[which]])
        GSL = [abuf[0][:, 0:512], abuf[0][:, 512:1024], abuf[1][:, 0:512], abuf[1][:, 512:1024],
               bbuf[0][:, 0:512], bbuf[0][:, 512:1024], bbuf[1][:, 0:512], bbuf[1][:, 512:1024],
               hbuf[0][:, 0:512], hbuf[0][:, 512:1024], hbuf[1][:, 0:512], hbuf[1][:, 512:1024],
               Btl[:, 2048:2560], Btl[:, 2560:3072], Btl[:, 3072:3584], Btl[:, 3584:4096]]
        tok_m = []
        gmdep = None
        for half in range(2):
            bank = 6
            deps = []
            for ot in range(8):
                deps.append(k.dma(SPQ, GSL[ot],
                                  c_grd[ot * 128:(ot + 1) * 128, half * 512:(half + 1) * 512],
                                  deps=[gmdep] if gmdep else []))
                deps.append(k.dma(SPQ, GSL[8 + ot],
                                  c_gnid[ot * 128:(ot + 1) * 128, half * 512:(half + 1) * 512],
                                  deps=[gmdep] if gmdep else []))
            mms = []
            for ot in range(8):
                mms.append(lambda p, ot=ot, bank=bank:
                           p.matmul(ps[bank][:], F_PT[:, ot * 128:(ot + 1) * 128], GSL[ot],
                                    start=(ot == 0), stop=False))
                mms.append(lambda p, ot=ot, bank=bank:
                           p.matmul(ps[bank][:], F_QT[:, ot * 128:(ot + 1) * 128], GSL[8 + ot],
                                    start=False, stop=(ot == 7)))

            def evac(p, half=half, bank=bank):
                return p.copy(F_M[:, half * 512:(half + 1) * 512], ps[bank][:])
            tok_m.append(k.mm_group(f"b{bank}", mms, "act", evac,
                                    extra_deps=deps + [prev_grp[0], prev_grp[1]]))
            gmdep = tok_m[-1]
        tw = k.dma(SPQ, ag0_in.ap(), F_M, deps=tok_m)
        tag = k.op("pool", lambda p: p.collective_compute(
            "AllGather", ALU.bypass, ins=[ag0_in.ap()], outs=[ag0_out.ap()],
            replica_groups=RG), deps=[tw], tok=True)
        return tag

    def ln_tile(src_ap, dst_ap, sl, eps, deps=()):
        c = lncol[:, 4 * sl:4 * sl + 4]
        dl = [d for d in deps if d is not None]
        t1 = k.op("dve", lambda p: p.reduce_sum(c[:, 0:1], src_ap, axis=AX.X), deps=dl, tok=True)
        t2 = k.op("act", lambda p: p.activation(out=zT[:, 0:1024], in_=src_ap, func=AF.Square,
                                                accum_out=c[:, 1:2]),
                  deps=dl + [st["zT_free"]], tok=True)
        k.op("act", lambda p: p.mul(c[:, 0:1], c[:, 0:1], cst[:, 3:4]), deps=[t1, t_cst])
        t3 = k.op("act", lambda p: p.mul(c[:, 1:2], c[:, 1:2], cst[:, 3:4]), deps=[t2], tok=True)
        t4 = k.op("dve", lambda p: p.tensor_tensor(c[:, 2:3], c[:, 0:1], c[:, 0:1], ALU.mult),
                  deps=[t3], tok=True)
        t5 = k.op("dve", lambda p: p.tensor_tensor(c[:, 2:3], c[:, 1:2], c[:, 2:3], ALU.subtract),
                  deps=[t4], tok=True)
        t6 = k.op("act", lambda p: p.activation(out=c[:, 2:3], in_=c[:, 2:3], func=AF.Ln,
                                                bias=cst[:, 0:1]), deps=[t5], tok=True)
        t7 = k.op("act", lambda p: p.activation(out=c[:, 2:3], in_=c[:, 2:3], func=AF.Exp,
                                                scale=cst[:, 2:3]), deps=[t6], tok=True)
        return k.op("dve", lambda p: p.tensor_scalar(dst_ap, src_ap, c[:, 0:1], c[:, 2:3],
                                                     ALU.subtract, ALU.mult), deps=[t7], tok=True)

    def rms_tile(xn_ap, sl, dep, nw_ap, nw_dep):
        c = lncol[:, 4 * sl + 3:4 * sl + 4]
        t2 = k.op("act", lambda p: p.activation(out=zT[:, 1024:2048], in_=xn_ap, func=AF.Square,
                                                accum_out=c), deps=[dep, st["zT_free"]], tok=True)
        t6 = k.op("act", lambda p: p.activation(out=c, in_=c, func=AF.Ln,
                                                scale=cst[:, 3:4], bias=cst[:, 1:2]), deps=[t2], tok=True)
        t7 = k.op("act", lambda p: p.activation(out=c, in_=c, func=AF.Exp, scale=cst[:, 2:3]),
                  deps=[t6], tok=True)
        t8 = k.op("dve", lambda p: p.tensor_scalar_mul(xn_ap, xn_ap, c), deps=[t7], tok=True)
        return k.op("dve", lambda p: p.tensor_tensor(xn_ap, xn_ap, nw_ap, ALU.mult),
                    deps=[t8, nw_dep], tok=True)

    def transpose_tile(src_ap, dst_ap, bank, deps):
        mms = [lambda p: p.transpose(ps[bank][:, 0:128], src_ap, eye[:])]

        def evac(p):
            return p.copy(dst_ap, ps[bank][:, 0:128])
        return k.mm_group(f"b{bank}", mms, "act", evac, extra_deps=list(deps) + [t_eye])

    def halo_xp(lay, dep_src):
        win = w_in_t[lay]
        c = lncol[0:3, 0:4]
        dl = [dep_src]
        t1 = k.op("dve", lambda p: p.reduce_sum(c[:, 0:1], hxs[0:3, :], axis=AX.X), deps=dl, tok=True)
        t2 = k.op("act", lambda p: p.activation(out=zT[0:3, 0:1024], in_=hxs[0:3, :], func=AF.Square,
                                                accum_out=c[:, 1:2]), deps=dl + [st["zT_free"]], tok=True)
        k.op("act", lambda p: p.mul(c[:, 0:1], c[:, 0:1], cst[0:3, 3:4]), deps=[t1, t_cst])
        t3 = k.op("act", lambda p: p.mul(c[:, 1:2], c[:, 1:2], cst[0:3, 3:4]), deps=[t2], tok=True)
        t4 = k.op("dve", lambda p: p.tensor_tensor(c[:, 2:3], c[:, 0:1], c[:, 0:1], ALU.mult),
                  deps=[t3], tok=True)
        t5 = k.op("dve", lambda p: p.tensor_tensor(c[:, 2:3], c[:, 1:2], c[:, 2:3], ALU.subtract),
                  deps=[t4], tok=True)
        t6 = k.op("act", lambda p: p.activation(out=c[:, 2:3], in_=c[:, 2:3], func=AF.Ln,
                                                bias=cst[0:3, 0:1]), deps=[t5], tok=True)
        t7 = k.op("act", lambda p: p.activation(out=c[:, 2:3], in_=c[:, 2:3], func=AF.Exp,
                                                scale=cst[0:3, 2:3]), deps=[t6], tok=True)
        tn = k.op("dve", lambda p: p.tensor_scalar(hxs[0:3, :], hxs[0:3, :], c[:, 0:1], c[:, 2:3],
                                                   ALU.subtract, ALU.mult), deps=[t7], tok=True)
        if lay == 1:
            t2b = k.op("act", lambda p: p.activation(out=zT[0:3, 1024:2048], in_=hxs[0:3, :], func=AF.Square,
                                                     accum_out=c[:, 3:4]), deps=[tn], tok=True)
            t6b = k.op("act", lambda p: p.activation(out=c[:, 3:4], in_=c[:, 3:4], func=AF.Ln,
                                                     scale=cst[0:3, 3:4], bias=cst[0:3, 1:2]), deps=[t2b], tok=True)
            t7b = k.op("act", lambda p: p.activation(out=c[:, 3:4], in_=c[:, 3:4], func=AF.Exp,
                                                     scale=cst[0:3, 2:3]), deps=[t6b], tok=True)
            tn2 = k.op("dve", lambda p: p.tensor_scalar_mul(hxs[0:3, :], hxs[0:3, :], c[:, 3:4]),
                       deps=[t7b], tok=True)
            tn = k.op("dve", lambda p: p.tensor_tensor(hxs[0:3, :], hxs[0:3, :], normw[0:3, :],
                                                       ALU.mult), deps=[tn2, st["nw_tok"]], tok=True)
        tlast = None
        for dmt in range(8):
            bank = 5
            mms = [lambda p, dmt=dmt: p.transpose(ps[bank][0:128, dmt * 4:dmt * 4 + 3],
                                                  hxs[0:3, dmt * 128:(dmt + 1) * 128],
                                                  eye[0:3, 0:3])]

            def evac(p, dmt=dmt):
                return p.copy(xnTh[:, dmt * 3:(dmt + 1) * 3], ps[bank][:, dmt * 4:dmt * 4 + 3])
            tlast = k.mm_group(f"b{bank}", mms, "act", evac, extra_deps=[tn, t_eye])
        for dt in range(NDT):
            par = dt % 2
            tws = k.dma(SPQ, s3(wb[par][:, 0:1024], 8),
                        r3(win[:, dt * 128:(dt + 1) * 128]),
                        deps=[st["wb_tok"][par]])
            bank = 6
            mms = []
            for dmt in range(8):
                mms.append(lambda p, dmt=dmt, par=par, bank=bank:
                           p.matmul(ps[bank][:, 0:3], wb[par][:, dmt * 128:(dmt + 1) * 128],
                                    xnTh[:, dmt * 3:(dmt + 1) * 3], start=(dmt == 0), stop=(dmt == 7)))

            def evac(p, dt=dt, bank=bank):
                return p.copy(xpT_v[:, dt * (T + 3):dt * (T + 3) + 3], ps[bank][:, 0:3])
            ct = k.mm_group(f"b{bank}", mms, "act", evac, extra_deps=[tws, tlast])
            st["wb_tok"][par] = ct
            st["xp_tail"][dt] = ct

    def emit_scan(lay, c, phaseB, d_dep, du_toks, bt_toks, ct_toks=None, hin_tok=None, lite=False):
        asb = A_sb[lay]
        csb = cc[lay]
        src_du = du_v if phaseB else uT
        dve_last = None
        for dt in range(NDT):
            for s in range(2):
                par = (dt * 2 + s) % 2
                guard = k.scan_guard.get(par)
                a_toks = []
                for n in range(N):
                    a_toks.append(k.op("act", lambda p, dt=dt, s=s, n=n, par=par: p.activation(
                        out=abuf[par][:, n * TS:(n + 1) * TS],
                        in_=dT[:, dt * T + s * TS: dt * T + (s + 1) * TS],
                        func=AF.Exp, scale=asb[:, dt * 16 + n:dt * 16 + n + 1]),
                        deps=[d_dep[dt], t_A[lay], guard] if n == 0 else [], tok=(n == N - 1)))
                b_toks = []
                for n in range(N):
                    b_toks.append(k.op("dve", lambda p, dt=dt, s=s, n=n, par=par: p.tensor_tensor(
                        bbuf[par][:, n * TS:(n + 1) * TS],
                        src_du[:, dt * T + s * TS: dt * T + (s + 1) * TS],
                        Btl[:, n * T + s * TS: n * T + s * TS + TS], ALU.mult),
                        deps=([du_toks[dt]] + bt_toks) if n == 0 else [], tok=(n == N - 1)))
                sc_toks = []
                for n in range(N):
                    if c == 0 and s == 0 and dt is not None and not phaseB:
                        init = 0.0
                    elif c == 0 and s == 0:
                        init = hin[:, dt * 16 + n:dt * 16 + n + 1]
                    elif s == 0:
                        init = carry[:, dt * 16 + n:dt * 16 + n + 1]
                    else:
                        init = hbuf[1 - par][:, n * TS + TS - 1:n * TS + TS]
                    dps = [a_toks[-1], b_toks[-1]] if n == 0 else []
                    if n == 0 and s == 0:
                        dps.append(st["carry_tok"][dt])
                    if n == 0 and phaseB and c == 0 and s == 0:
                        dps.append(hin_tok)
                    sc_toks.append(k.op("dve", lambda p, n=n, par=par, init=init: p.tensor_tensor_scan(
                        hbuf[par][:, n * TS:(n + 1) * TS],
                        abuf[par][:, n * TS:(n + 1) * TS],
                        bbuf[par][:, n * TS:(n + 1) * TS],
                        init, ALU.mult, ALU.add), deps=dps, tok=(n == N - 1)))
                k.scan_guard[par] = sc_toks[-1]
                if s == 1:
                    src = hbuf[par][:].rearrange("p (n t) -> p n t", n=N)[:, :, TS - 1:TS].squeeze(2)
                    st["carry_tok"][dt] = k.op("dve", lambda p, dt=dt, src=src: p.tensor_copy(
                        carry[:, dt * 16:(dt + 1) * 16], src), deps=[sc_toks[-1]], tok=True)
                if phaseB and not lite:
                    h3 = hbuf[par][:].rearrange("p (n t) -> p n t", n=N)
                    a3 = abuf[par][:].rearrange("p (n t) -> p n t", n=N)
                    c3 = Ctl[:].rearrange("p (n t) -> p n t", n=N)[:, :, s * TS:(s + 1) * TS]
                    tm = k.op("dve", lambda p, h3=h3, a3=a3, c3=c3: p.tensor_tensor(
                        a3, h3, c3, ALU.mult),
                        deps=(ct_toks or []) + [sc_toks[-1]], tok=True)
                    k.scan_guard[par] = tm
                    ty = k.op("dve", lambda p, s=s, par=par: p.tensor_reduce(
                        ycol[:, s * TS:(s + 1) * TS],
                        abuf[par][:].rearrange("p (n t) -> p t n", n=N),
                        axis=AX.X, op=ALU.add),
                        deps=[tm, st["gcol_free"] if s == 0 else None], tok=True)
                    k.scan_guard[par] = ty
                    dve_last = ty
            if phaseB and not lite:
                tg1 = k.op("act", lambda p, dt=dt: p.activation(
                    out=gcol[:], in_=zT[:, dt * T:(dt + 1) * T], func=AF.Silu),
                    deps=[st["zT_load"], st["gcol_free"]], tok=True)
                tg2 = k.op("dve", lambda p, dt=dt: p.scalar_tensor_tensor(
                    ycol[:], uT[:, dt * T:(dt + 1) * T], csb[:, 80 + dt:80 + dt + 1],
                    ycol[:], ALU.mult, ALU.add), deps=[dve_last, st["uT_load"], t_cc[lay]], tok=True)
                tg3 = k.op("dve", lambda p, dt=dt: p.tensor_tensor(
                    ygT_v[:, dt * T:(dt + 1) * T], ycol[:], gcol[:], ALU.mult),
                    deps=[tg1, tg2, st["ygT_free"]] +
                         (list(st["xp_tail"]) if dt == 0 else []), tok=True)
                st["gcol_free"] = tg3
                st["yg_tok"][dt] = tg3
        return k.scan_guard.get(0), k.scan_guard.get(1)

    def mamba_phaseA(lay):
        win = w_in_t[lay]
        wxt = w_x_t[lay]
        spbc = sp_bc[lay]
        csb = cc[lay]
        nxp = 96 if lay == 0 else 33
        st["stot_z"] = k.dma(SPQ, Stot[:], c_zeros[:, 0:16], deps=[st.get("stot_free")])
        if lay == 1:
            st["nw_tok"] = k.dma(PLQ, normw[:], w1_normw.partition_broadcast(128),
                                 deps=[st["ctl_free"]] if False else [])
        if lay == 0:
            th = k.dma(SPQ, hxs[0:3, :], x_in[0:3, :])
        else:
            tg = k.op("pool", lambda p: p.collective_compute(
                "AllGather", ALU.bypass, ins=[ag2_in.ap()], outs=[ag2_out.ap()],
                replica_groups=RG), deps=[st["ag2src"]], tok=True)
            tl = k.dma(SPQ, hxs[0:24, :], ag2_out.ap(), deps=[tg])
            th = None
            for half in range(2):
                bank = 4
                mms = [lambda p, half=half, bank=bank:
                       p.matmul(ps[bank][0:3, :], sel3[:], hxs[0:24, half * 512:(half + 1) * 512],
                                start=True, stop=True)]

                def evac(p, half=half, bank=bank):
                    return p.copy(hxs[0:3, half * 512:(half + 1) * 512], ps[bank][0:3, :])
                th = k.mm_group(f"b{bank}", mms, "act", evac, extra_deps=[tl, t_sel3])
        halo_xp(lay, th)

        for c in range(NCH):
            if lay == 0:
                tx = k.dma(SPQ, s3(xwin[:], 2),
                           r3(x_in[3 + c * T: 3 + (c + 1) * T, :]),
                           deps=[st["xw_free"]])
            else:
                tx = k.dma(SPQ, s3(xwin[:], 2),
                           r3(xr.ap()[c * T:(c + 1) * T, :]),
                           deps=[st["xr_tok"][c], st["xw_free"]])
            ttr_toks = []
            resid_toks = []
            for tt in range(2):
                t_ln = ln_tile(xwin[:, tt * 1024:(tt + 1) * 1024], xn_v, tt, 1e-5,
                               deps=[tx, st["xn_free"]])
                if lay == 1:
                    tr = k.op("dve", lambda p, tt=tt: p.tensor_tensor(
                        xwin[:, tt * 1024:(tt + 1) * 1024], xwin[:, tt * 1024:(tt + 1) * 1024],
                        xn_v, ALU.add), deps=[t_ln], tok=True)
                    resid_toks.append(tr)
                    t_ln = rms_tile(xn_v, tt, tr, normw[:], st["nw_tok"])
                for dmt in range(8):
                    tt_tok = transpose_tile(xn_v[:, dmt * 128:(dmt + 1) * 128],
                                            xnT_v[:, dmt * 256 + tt * 128: dmt * 256 + (tt + 1) * 128],
                                            2 + dmt % 2, [t_ln, st["xnT_free"]])
                st["xn_free"] = tt_tok
                ttr_toks.append(tt_tok)
            if lay == 1:
                st["xr_tok"][c] = k.dma(SPQ, r3(xr.ap()[c * T:(c + 1) * T, :]),
                                        s3(xwin[:], 2), deps=resid_toks)
                st["xw_free"] = st["xr_tok"][c]
            else:
                st["xw_free"] = ttr_toks[-1]
            z_toks = []
            xz_last = None
            for j in range(2 * NDT):
                par = j % 2
                tws = k.dma(SPQ, s3(wb[par][:, 0:1024], 8),
                            r3(win[:, j * 128:(j + 1) * 128]),
                            deps=[st["wb_tok"][par]])
                bank = par
                mms = []
                for dmt in range(8):
                    mms.append(lambda p, dmt=dmt, par=par, bank=bank:
                               p.matmul(ps[bank][:, 0:T], wb[par][:, dmt * 128:(dmt + 1) * 128],
                                        xnT_v[:, dmt * 256:(dmt + 1) * 256],
                                        start=(dmt == 0), stop=(dmt == 7)))
                if j < NDT:
                    def evac(p, j=j, bank=bank):
                        return p.copy(xpT_v[:, j * (T + 3) + 3:(j + 1) * (T + 3)], ps[bank][:, 0:T])
                    dcopy = [st["xp_tail"][j]]
                else:
                    def evac(p, j=j, bank=bank):
                        return p.copy(zT[:, (j - NDT) * T:(j - NDT + 1) * T], ps[bank][:, 0:T])
                    dcopy = [st["zT_free"]]
                ct = k.mm_group(f"b{bank}", mms, "act", evac,
                                extra_deps=[tws] + ttr_toks + dcopy)
                st["wb_tok"][par] = ct
                xz_last = ct
                if j < NDT:
                    st["xp_evac"][j] = ct
                else:
                    z_toks.append(ct)
            st["xnT_free"] = xz_last
            tspz = k.dma(SPQ, r3(sp_z.ap()[:, c * T:(c + 1) * T]),
                         s3(zT[:], 16), deps=z_toks)
            st["z_sp"][lay][c] = tspz
            st["zT_free"] = tspz
            u_toks = []
            prev_silu = None
            for dt in range(NDT):
                base = dt * (T + 3)
                t1 = k.op("dve", lambda p, base=base, dt=dt: p.tensor_scalar_mul(
                    tcol[:], xpT_v[:, base:base + T], csb[:, dt * 4:dt * 4 + 1]),
                    deps=[st["xp_evac"][dt], t_cc[lay], prev_silu], tok=True)
                for kk in range(1, 4):
                    t1 = k.op("dve", lambda p, base=base, dt=dt, kk=kk: p.scalar_tensor_tensor(
                        tcol[:], xpT_v[:, base + kk:base + kk + T], csb[:, dt * 4 + kk:dt * 4 + kk + 1],
                        tcol[:], ALU.mult, ALU.add), deps=[t1], tok=True)
                tu = k.op("act", lambda p, dt=dt: p.activation(
                    out=uT[:, dt * T:(dt + 1) * T], in_=tcol[:], func=AF.Silu,
                    bias=csb[:, 64 + dt:64 + dt + 1]), deps=[t1, st["uT_free"]], tok=True)
                prev_silu = tu
                u_toks.append(tu)
                st["xp_tail"][dt] = k.op("dve", lambda p, base=base: p.tensor_copy(
                    xpT_v[:, base:base + 3], xpT_v[:, base + T:base + T + 3]), deps=[tu], tok=True)
            tspu = k.dma(SPQ, r3(sp_u.ap()[:, c * T:(c + 1) * T]),
                         s3(uT[:], 16), deps=u_toks)
            st["u_sp"][lay][c] = tspu
            twx = k.dma(SPQ, s3(wb[0][:, 0:16 * nxp], 16),
                        r3(wxt), deps=[st["wb_tok"][0]])
            bank = 4
            mms = []
            for dt in range(NDT):
                mms.append(lambda p, dt=dt, bank=bank, nxp=nxp:
                           p.matmul(ps[bank][0:nxp, 0:T], wb[0][:, dt * nxp:(dt + 1) * nxp],
                                    uT[:, dt * T:(dt + 1) * T], start=(dt == 0), stop=(dt == NDT - 1)))

            def evacx(p, nxp=nxp, bank=bank):
                return p.copy(xdbl[0:nxp, 0:T], ps[bank][0:nxp, 0:T])
            tx2 = k.mm_group(f"b{bank}", mms, "act", evacx,
                             extra_deps=u_toks + [twx, st["xdbl_free"]] + st["bc_sp"])
            st["wb_tok"][0] = tx2
            if lay == 0:
                tb = k.dma(PLQ, spbc.ap()[0:16, c * T:(c + 1) * T], xdbl[64:80, 0:T], deps=[tx2])
                tc2 = k.dma(PLQ, spbc.ap()[16:32, c * T:(c + 1) * T], xdbl[80:96, 0:T], deps=[tx2])
                tdr = None
            else:
                tb = k.dma(PLQ, spbc.ap()[0:16, c * T:(c + 1) * T], xdbl[1:17, 0:T], deps=[tx2])
                tc2 = k.dma(PLQ, spbc.ap()[16:32, c * T:(c + 1) * T], xdbl[17:33, 0:T], deps=[tx2])
                tdr = k.dma(PLQ, spbc.ap()[32:33, c * T:(c + 1) * T], xdbl[0:1, 0:T], deps=[tx2])
            st["bc_sp"] = [tb, tc2] + ([tdr] if tdr else [])
            bt_toks = []
            for n in range(N):
                bt_toks.append(k.dma(PLQ, Btl[:, n * T:(n + 1) * T],
                                     spbc.ap()[n:n + 1, c * T:(c + 1) * T].partition_broadcast(128),
                                     deps=[tb, st["btl_free"]] if n == 0 else [tb]))
            d_toks = []
            if lay == 0:
                twd = k.dma(SPQ, wb[1][0:64, 0:2048], w0_dt_t, deps=[st["wb_tok"][1]])
                for dt in range(NDT):
                    bank = 5 + (dt % 2)
                    mms = [lambda p, dt=dt, bank=bank:
                           p.matmul(ps[bank][:, 0:T], wb[1][0:64, dt * 128:(dt + 1) * 128],
                                    xdbl[0:64, 0:T], start=True, stop=True)]

                    def evacd(p, dt=dt, bank=bank):
                        return p.activation(out=dT[:, dt * T:(dt + 1) * T], in_=ps[bank][:, 0:T],
                                            func=AF.Exp, bias=csb[:, 96 + dt:96 + dt + 1])
                    ct = k.mm_group(f"b{bank}", mms, "act", evacd,
                                    extra_deps=[tx2, twd, t_cc[0], st["dT_free"] if dt == 0 else None])
                    td = k.op("act", lambda p, dt=dt: p.activation(
                        out=dT[:, dt * T:(dt + 1) * T], in_=dT[:, dt * T:(dt + 1) * T],
                        func=AF.Ln, bias=1.0), deps=[ct], tok=True)
                    d_toks.append(td)
                st["wb_tok"][1] = d_toks[-1]
                st["xdbl_free"] = d_toks[-1]
                for tcp in (tb, tc2):
                    pass
            else:
                tdb = k.dma(PLQ, dtrawb[:, 0:T],
                            spbc.ap()[32:33, c * T:(c + 1) * T].partition_broadcast(128),
                            deps=[tdr, st["gcol_free"]])
                for dt in range(NDT):
                    te = k.op("act", lambda p, dt=dt: p.activation(
                        out=dT[:, dt * T:(dt + 1) * T], in_=dtrawb[:, 0:T], func=AF.Exp,
                        scale=csb[:, 112 + dt:112 + dt + 1], bias=csb[:, 96 + dt:96 + dt + 1]),
                        deps=[tdb, t_wdt1, t_cc[1], st["dT_free"] if dt == 0 else None], tok=True)
                    td = k.op("act", lambda p, dt=dt: p.activation(
                        out=dT[:, dt * T:(dt + 1) * T], in_=dT[:, dt * T:(dt + 1) * T],
                        func=AF.Ln, bias=1.0), deps=[te], tok=True)
                    d_toks.append(td)
                st["xdbl_free"] = tdb
            tspd = k.dma(SPQ, r3(sp_d.ap()[:, c * T:(c + 1) * T]),
                         s3(dT[:], 16), deps=d_toks)
            st["d_sp"][lay][c] = tspd
            du_toks = []
            for dt in range(NDT):
                du_toks.append(k.op("dve", lambda p, dt=dt: p.tensor_tensor(
                    uT[:, dt * T:(dt + 1) * T], uT[:, dt * T:(dt + 1) * T],
                    dT[:, dt * T:(dt + 1) * T], ALU.mult),
                    deps=[d_toks[dt], tspu, tx2], tok=True))
            stot_last = st.get("stot_last")
            for dt in range(NDT):
                t1 = k.op("dve", lambda p, dt=dt: p.reduce_sum(
                    lncol[:, 15:16], dT[:, dt * T:(dt + 1) * T], axis=AX.X),
                    deps=[d_toks[dt], stot_last], tok=True)
                stot_last = k.op("dve", lambda p, dt=dt: p.tensor_tensor(
                    Stot[:, dt:dt + 1], Stot[:, dt:dt + 1], lncol[:, 15:16], ALU.add),
                    deps=[t1, st["stot_z"]] if c == 0 else [t1], tok=True)
            st["stot_last"] = stot_last
            g0, g1 = emit_scan(lay, c, False, d_toks, du_toks, bt_toks)
            st["btl_free"] = g1 or g0
            st["uT_free"] = g1 or g0
            st["dT_free"] = stot_last
        dt_toks = []
        for dt in range(NDT):
            dt_toks.append(k.op("act", lambda p, dt=dt: p.activation(
                out=Dtot[:, dt * 16:(dt + 1) * 16], in_=A_sb[lay][:, dt * 16:(dt + 1) * 16],
                func=AF.Exp, scale=Stot[:, dt:dt + 1]), deps=[t_A[lay], st["dT_free"]], tok=True))
        st["stot_free"] = dt_toks[-1]
        agi, ago = ag_in[lay], ag_out[lay]
        t1 = k.dma(SPQ, agi.ap()[:, 0:256], Dtot[:], deps=dt_toks)
        t2 = k.dma(SPQ, agi.ap()[:, 256:512], carry[:],
                   deps=[st["carry_tok"][dt] for dt in range(NDT)])
        tg = k.op("pool", lambda p, agi=agi, ago=ago: p.collective_compute(
            "AllGather", ALU.bypass, ins=[agi.ap()], outs=[ago.ap()],
            replica_groups=RG), deps=[t1, t2], tok=True)
        tl = k.dma(SPQ, s3(agsb, 8), ago.ap().rearrange("(j p) c -> p j c", p=128),
                   deps=[tg, st["ygT_free"]] + list(st["xp_tail"]) + [st["xn_free"], st["xnT_free"]])
        tz = k.dma(SPQ, hin[:], c_zeros, deps=[st.get("hin_free")])
        tprev = tz
        for j in range(NC - 1):
            Dj = agsb[:, j * 512:j * 512 + 256]
            hj = agsb[:, j * 512 + 256:j * 512 + 512]
            ta = k.op("dve", lambda p, Dj=Dj: p.tensor_tensor(Dtot[:], Dj, hin[:], ALU.mult),
                      deps=[tl, tprev], tok=True)
            tb2 = k.op("dve", lambda p, hj=hj: p.tensor_tensor(Dtot[:], Dtot[:], hj, ALU.add),
                       deps=[ta], tok=True)
            tc3 = k.op("dve", lambda p: p.tensor_tensor(Dtot[:], Dtot[:], hin[:], ALU.subtract),
                       deps=[tb2], tok=True)
            tprev = k.op("dve", lambda p, j=j: p.scalar_tensor_tensor(
                hin[:], Dtot[:], chm[:, j:j + 1], hin[:], ALU.mult, ALU.add),
                deps=[tc3, t_chm], tok=True)
        st["ygT_free"] = tprev
        return tprev

    def mamba_phaseB(lay, hin_tok):
        PB_LVL = int(os.environ.get("MK_PB", "9"))
        wout = w_out[lay]
        spbc = sp_bc[lay]
        for c in range(NCH):
            last_sp = [st["u_sp"][lay][NCH - 1], st["d_sp"][lay][NCH - 1],
                       st["z_sp"][lay][NCH - 1]] if c == 0 else []
            tu = k.dma(SPQ, s3(uT[:], 16), r3(sp_u.ap()[:, c * T:(c + 1) * T]),
                       deps=[st["u_sp"][lay][c], st["uT_free"]] + last_sp)
            td = k.dma(SPQ, s3(dT[:], 16), r3(sp_d.ap()[:, c * T:(c + 1) * T]),
                       deps=[st["d_sp"][lay][c], st["dT_free"]] + last_sp)
            tz = k.dma(SPQ, s3(zT[:], 16), r3(sp_z.ap()[:, c * T:(c + 1) * T]),
                       deps=[st["z_sp"][lay][c], st["zT_free"]] + last_sp)
            st["uT_load"], st["zT_load"] = tu, tz
            if lay == 0:
                tx = k.dma(SPQ, s3(xwin[:], 2),
                           r3(x_in[3 + c * T: 3 + (c + 1) * T, :]),
                           deps=[st["xw_free"]])
            else:
                tx = k.dma(SPQ, s3(xwin[:], 2),
                           r3(xr.ap()[c * T:(c + 1) * T, :]),
                           deps=[st["xr_tok"][c], st["xw_free"]])
            bt_toks, ct_toks = [], []
            for n in range(N):
                bt_toks.append(k.dma(PLQ, Btl[:, n * T:(n + 1) * T],
                                     spbc.ap()[n:n + 1, c * T:(c + 1) * T].partition_broadcast(128),
                                     deps=[st["btl_free"]] if n == 0 else []))
                ct_toks.append(k.dma(PLQ, Ctl[:, n * T:(n + 1) * T],
                                     spbc.ap()[16 + n:17 + n, c * T:(c + 1) * T].partition_broadcast(128),
                                     deps=[st["ctl_free"]] if n == 0 else []))
            du_toks = []
            for dt in range(NDT):
                dps = [tu, td, st["du_free"]] if dt == 0 else []
                if c == 0 and dt == 0:
                    dps += [st["xn_free"], st["xnT_free"]]
                du_toks.append(k.op("dve", lambda p, dt=dt: p.tensor_tensor(
                    du_v[:, dt * T:(dt + 1) * T], uT[:, dt * T:(dt + 1) * T],
                    dT[:, dt * T:(dt + 1) * T], ALU.mult), deps=dps, tok=True))
            if PB_LVL < 2:
                st["btl_free"] = du_toks[-1]
                st["ctl_free"] = du_toks[-1]
                st["du_free"] = du_toks[-1]
                st["uT_free"] = du_toks[-1]
                st["dT_free"] = du_toks[-1]
                st["zT_free"] = tz
                st["xw_free"] = tx
                continue
            g0, g1 = emit_scan(lay, c, True, [td] * NDT, du_toks, bt_toks,
                               ct_toks=ct_toks, hin_tok=hin_tok,
                               lite=(PB_LVL < 3))
            st["btl_free"] = g1 or g0
            st["ctl_free"] = g1 or g0
            st["du_free"] = g1 or g0
            st["uT_free"] = st["yg_tok"][NDT - 1] if PB_LVL >= 3 else (g1 or g0)
            st["dT_free"] = g1 or g0
            st["zT_free"] = st["yg_tok"][NDT - 1] if PB_LVL >= 3 else tz
            if PB_LVL < 4:
                st["xw_free"] = tx
                continue
            pe_last = None
            for dt in range(NDT):
                par = dt % 2
                tws = k.dma(SPQ, wb[par][:, 0:1024], wout[dt * 128:(dt + 1) * 128, :],
                            deps=[st["wb_tok"][par]])
                for g, (tt2, half) in enumerate(((0, 0), (0, 1), (1, 0), (1, 1))):
                    dep0 = [tws, st["yg_tok"][dt]]
                    if dt == 0:
                        dep0.append(k.bank_tok.get(f"b{g}"))
                    pe_last = k.op("pe", lambda p, dt=dt, par=par, g=g, tt2=tt2, half=half:
                                   p.matmul(ps[g][:],
                                            ygT_v[:, dt * T + tt2 * TS: dt * T + (tt2 + 1) * TS],
                                            wb[par][:, half * 512:(half + 1) * 512],
                                            start=(dt == 0), stop=(dt == NDT - 1)),
                                   deps=dep0, tok=(g == 3))
                st["wb_tok"][par] = pe_last
            ev_toks = []
            for g, (tt2, half) in enumerate(((0, 0), (0, 1), (1, 0), (1, 1))):
                te = k.op("dve", lambda p, g=g, tt2=tt2, half=half: p.tensor_tensor(
                    xwin[:, tt2 * 1024 + half * 512: tt2 * 1024 + (half + 1) * 512],
                    xwin[:, tt2 * 1024 + half * 512: tt2 * 1024 + (half + 1) * 512],
                    ps[g][:], ALU.add), deps=[pe_last, tx], tok=True)
                k.bank_tok[f"b{g}"] = te
                ev_toks.append(te)
            st["xr_tok"][c] = k.dma(SPQ, r3(xr.ap()[c * T:(c + 1) * T, :]),
                                    s3(xwin[:], 2), deps=ev_toks)
            st["xw_free"] = st["xr_tok"][c]
            st["ygT_free"] = pe_last

    def l0_ffn(tok_fold):
        tm = [k.dma(SPQ, s3(Mres[0][:], 4), r3(ag0_out.ap()[0:512, :]),
                    deps=[tok_fold, st["btl_free"]]),
              k.dma(SPQ, s3(Mres[1][:], 4), r3(ag0_out.ap()[512:1024, :]),
                    deps=[st["ctl_free"]])]
        tbia = k.dma(SPQ, f0b[:], f0_b)

        def mslice(dmt, half):
            buf = Mres[dmt // 4]
            i = dmt % 4
            return buf[:, i * 1024 + half * 512: i * 1024 + (half + 1) * 512]
        for c in range(NCH):
            tx = k.dma(SPQ, s3(xwin[:], 2),
                       r3(xr.ap()[c * T:(c + 1) * T, :]),
                       deps=[st["xr_tok"][c], st["xw_free"]])
            ttr_toks = []
            ln_toks = []
            for tt in range(2):
                t_ln = ln_tile(xwin[:, tt * 1024:(tt + 1) * 1024], xn_v, tt, 1e-5,
                               deps=[tx, st["xn_free"]])
                ln_toks.append(t_ln)
                for dmt in range(8):
                    tt_tok = transpose_tile(xn_v[:, dmt * 128:(dmt + 1) * 128],
                                            xnT_v[:, dmt * 256 + tt * 128: dmt * 256 + (tt + 1) * 128],
                                            4 + dmt % 2, [t_ln, st["xnT_free"]])
                st["xn_free"] = tt_tok
                ttr_toks.append(tt_tok)
            ev_toks = []
            for g, (tt2, half) in enumerate(((0, 0), (0, 1), (1, 0), (1, 1))):
                bank = g
                mms = []
                for dmt in range(8):
                    mms.append(lambda p, dmt=dmt, tt2=tt2, half=half, bank=bank:
                               p.matmul(ps[bank][:],
                                        xnT_v[:, dmt * 256 + tt2 * 128: dmt * 256 + (tt2 + 1) * 128],
                                        mslice(dmt, half), start=(dmt == 0), stop=False))
                mms.append(lambda p, half=half, bank=bank:
                           p.matmul(ps[bank][:], onesr[0:1, :], f0b[0:1, half * 512:(half + 1) * 512],
                                    start=False, stop=True))

                def evac(p, tt2=tt2, half=half, bank=bank):
                    return p.tensor_tensor(
                        xwin[:, tt2 * 1024 + half * 512: tt2 * 1024 + (half + 1) * 512],
                        xwin[:, tt2 * 1024 + half * 512: tt2 * 1024 + (half + 1) * 512],
                        ps[bank][:], ALU.add)
                ev_toks.append(k.mm_group(f"b{bank}", mms, "dve", evac,
                                          extra_deps=tm + [tbia, t_ones] + ttr_toks))
            st["xnT_free"] = ev_toks[-1]
            st["xr_tok"][c] = k.dma(SPQ, r3(xr.ap()[c * T:(c + 1) * T, :]),
                                    s3(xwin[:], 2), deps=ev_toks)
            st["xw_free"] = st["xr_tok"][c]
            if c == NCH - 1:
                st["ag2src"] = k.dma(SPQ, ag2_in.ap(), xwin[125:128, 1024:2048], deps=ev_toks)
                st["xw_free"] = st["ag2src"]
        st["btl_free"] = ev_toks[-1]
        st["ctl_free"] = ev_toks[-1]

    def moe():
        tgw = k.dma(SPQ, s3(gwsb[:], 8), r3(m_gw_t))
        prev_sp = None
        gate_toks = []
        for c in range(NCH):
            tx = k.dma(SPQ, s3(xw2[:], 2),
                       r3(xr.ap()[c * T:(c + 1) * T, :]),
                       deps=[st["xr_tok"][c], st.get("xw2_free"),
                             st["du_free"] if c == 0 else None, st["ygT_free"] if c == 0 else None])
            ttr_toks = []
            for tt in range(2):
                t_ln = ln_tile(xw2[:, tt * 1024:(tt + 1) * 1024], xn2, tt, 1e-5,
                               deps=[tx, st["xn2_free"] if "xn2_free" in st else None,
                                     st["d_sp"][1][NCH - 1] if c == 0 and tt == 0 else None])
                for dmt in range(8):
                    tt_tok = transpose_tile(xn2[:, dmt * 128:(dmt + 1) * 128],
                                            xnT2[:, dmt * 256 + tt * 128: dmt * 256 + (tt + 1) * 128],
                                            2 + dmt % 2, [t_ln, prev_sp])
                st["xn2_free"] = tt_tok
                ttr_toks.append(tt_tok)
            prev_sp = k.dma(SPQ, r3(sp_xnt.ap()[:, c * T:(c + 1) * T]),
                            s3(xnT2[:], 8), deps=ttr_toks)
            st["xnt_sp"] = prev_sp
            st["xn2_free"] = prev_sp
            for tt in range(2):
                bank = 6
                mms = []
                for dmt in range(8):
                    mms.append(lambda p, dmt=dmt, tt=tt, bank=bank:
                               p.matmul(ps[bank][:, 0:2],
                                        xnT2[:, dmt * 256 + tt * 128: dmt * 256 + (tt + 1) * 128],
                                        gwsb[:, dmt * 2:(dmt + 1) * 2], start=(dmt == 0), stop=(dmt == 7)))

                def evg(p, bank=bank):
                    return p.copy(tcol[:, 0:2], ps[bank][:, 0:2])
                tge = k.mm_group(f"b{bank}", mms, "act", evg,
                                 extra_deps=ttr_toks + [tgw] +
                                 ([gate_toks[-1]] if gate_toks else [st["gcol_free"]]))
                gate_toks.append(k.op("dve", lambda p, c=c, tt=tt: p.tensor_tensor(
                    m0buf[:, c * 2 + tt:c * 2 + tt + 1],
                    tcol[:, 0:1], tcol[:, 1:2], ALU.is_ge), deps=[tge], tok=True))
        bank = 7
        mms = [lambda p: p.transpose(ps[bank][0:16, 0:128], m0buf[:], eye[:])]

        def evm(p):
            return p.copy(hxs[0:16, 0:128], ps[bank][0:16, 0:128])
        tev = k.mm_group(f"b{bank}", mms, "act", evm, extra_deps=gate_toks + [t_eye])
        tw0s = []
        for r16 in range(16):
            tw0s.append(k.dma(SPQ, sp_m0.ap()[0:1, r16 * 128:(r16 + 1) * 128],
                              hxs[r16:r16 + 1, 0:128], deps=[tev] if r16 == 0 else []))
        tmb = k.dma(PLQ, m0til, sp_m0.ap().partition_broadcast(128),
                    deps=tw0s + [st["du_free"]])
        m1til = pv(43008, 2048)
        tmb1 = k.op("dve", lambda p: p.tensor_scalar(
            m1til, m0til, cst[:, 5:6], cst[:, 6:7], ALU.mult, ALU.add),
            deps=[tmb, t_cst, st["wb_tok"][0], st["wb_tok"][1]], tok=True)
        tk = [k.dma(SPQ, kbuf[:, 0:32], m_b1[0][0].rearrange("(a p) -> p a", p=128), slow=True),
              k.dma(SPQ, kbuf[:, 32:64], m_b1[1][0].rearrange("(a p) -> p a", p=128), slow=True),
              k.dma(SPQ, kbuf[:, 64:96], m_b1[2][0].rearrange("(a p) -> p a", p=128), slow=True)]
        tb1c = k.op("dve", lambda p: p.tensor_copy(b1col[:], kbuf[:]), deps=tk, tok=True)
        tkr = k.op("act", lambda p: p.activation(out=kbuf[:, 0:64], in_=kbuf[:, 0:64], func=AF.Relu),
                   deps=[tb1c], tok=True)

        w1_tok = [None, None, None]
        w2_tok = [None, None, None]
        ksb_tok = None
        xnw_tok = None
        hb_free = [None, None, None]
        last_evh = [None, None, None]
        yacc_last = None
        for hb in range(NHB):
            for f in range(3):
                g0dep = ([k.scan_guard.get(0), k.scan_guard.get(1)] if hb == 0 else
                         [last_evh[f], yacc_last, ksb_tok])
                w1_tok[f] = k.dma(SPQ, s3(w1s[f][:], 8),
                                  r3(m_w1[f][:, hb * HB:(hb + 1) * HB]),
                                  deps=[w1_tok[f]] + g0dep)
                w2_tok[f] = k.dma(SPQ, s3(w2s[f][:], 2),
                                  r3(m_w2[f][hb * HB:(hb + 1) * HB, :]),
                                  deps=[w2_tok[f]] + g0dep)
            for e in range(2):
                for half in range(2):
                    bank = 6
                    mms = []
                    for ht in range(2):
                        mms.append(lambda p, e=e, ht=ht, half=half, hb=hb, bank=bank:
                                   p.matmul(ps[bank][0:1, 0:512],
                                            kbuf[:, e * 32 + hb * 2 + ht: e * 32 + hb * 2 + ht + 1],
                                            w2s[e][:, ht * 1024 + half * 512: ht * 1024 + (half + 1) * 512],
                                            start=(ht == 0), stop=(ht == 1)))

                    def evk(p, e=e, half=half, hb=hb, bank=bank):
                        dst = ksb[0:1, e * 1024 + half * 512:e * 1024 + (half + 1) * 512]
                        if hb == 0:
                            return p.tensor_copy(dst, ps[bank][0:1, 0:512])
                        return p.tensor_tensor(dst, dst, ps[bank][0:1, 0:512], ALU.add)
                    ksb_tok = k.mm_group(f"b{bank}", mms, "dve", evk,
                                         extra_deps=[tkr, w2_tok[e], ksb_tok] +
                                                    ([gate_toks[-1]] if hb == 0 else []))
            for tc in range(4):
                txn = k.dma(SPQ, s3(xnw, 8),
                            r3(sp_xnt.ap()[:, tc * 512:(tc + 1) * 512]),
                            deps=[st["xnt_sp"], xnw_tok] +
                                 ([st["ygT_free"], st["du_free"], k.scan_guard.get(0),
                                   k.scan_guard.get(1)] if hb == 0 and tc == 0 else []))
                h_toks = []
                for f in range(3):
                    for ht in range(2):
                        bank = ht
                        mms = []
                        for dmt in range(8):
                            mms.append(lambda p, f=f, ht=ht, dmt=dmt, bank=bank:
                                       p.matmul(ps[bank][:],
                                                w1s[f][:, dmt * HB + ht * 128: dmt * HB + (ht + 1) * 128],
                                                xnw[:, dmt * 512:(dmt + 1) * 512],
                                                start=(dmt == 0), stop=(dmt == 7)))

                        def evh(p, f=f, ht=ht, hb=hb, bank=bank):
                            return p.activation(out=hbv[f][:, ht * 512:(ht + 1) * 512], in_=ps[bank][:],
                                                func=AF.Relu,
                                                bias=b1col[:, f * 32 + hb * 2 + ht: f * 32 + hb * 2 + ht + 1])
                        ct = k.mm_group(f"b{bank}", mms, "act", evh,
                                        extra_deps=[txn, w1_tok[f], tb1c, hb_free[f]] +
                                        ([st["btl_free"], st["ctl_free"]] if hb == 0 and tc == 0 else []))
                        last_evh[f] = ct
                        h_toks.append(ct)
                mk_toks = []
                for f in range(2):
                    mt = m0til if f == 0 else m1til
                    for ht in range(2):
                        mk_toks.append(k.op("dve", lambda p, f=f, ht=ht, tc=tc, mt=mt: p.tensor_tensor(
                            hbv[f][:, ht * 512:(ht + 1) * 512], hbv[f][:, ht * 512:(ht + 1) * 512],
                            mt[:, tc * 512:(tc + 1) * 512], ALU.mult),
                            deps=[h_toks[f * 2 + ht], tmb, tmb1], tok=True))
                hdeps = mk_toks + h_toks[4:6]
                for dmt in range(8):
                    bank = 2 + (dmt % 2)
                    mms = []
                    for f in range(3):
                        for ht in range(2):
                            mms.append(lambda p, f=f, ht=ht, dmt=dmt, bank=bank:
                                       p.matmul(ps[bank][:],
                                                w2s[f][:, ht * 1024 + dmt * 128: ht * 1024 + dmt * 128 + 128],
                                                hbv[f][:, ht * 512:(ht + 1) * 512],
                                                start=(f == 0 and ht == 0), stop=(f == 2 and ht == 1)))

                    def evy(p, dmt=dmt, tc=tc, hb=hb, bank=bank):
                        dst = y_acc[:, dmt * 2048 + tc * 512: dmt * 2048 + (tc + 1) * 512]
                        if hb == 0:
                            return p.tensor_copy(dst, ps[bank][:])
                        return p.tensor_tensor(dst, dst, ps[bank][:], ALU.add)
                    yacc_last = k.mm_group(f"b{bank}", mms, "dve", evy,
                                           extra_deps=hdeps + [w2_tok[0], w2_tok[1], w2_tok[2]] +
                                           ([st["xw_free"], st["uT_free"], st["zT_free"],
                                             st["btl_free"], st["dT_free"]] if hb == 0 else []))
                hb_free[0] = hb_free[1] = hb_free[2] = yacc_last
                xnw_tok = yacc_last
        gdep = [gate_toks[-1], prev_sp]
        tb2 = [k.dma(SPQ, crow[0:1, 0:1024], m_b2e0, deps=gdep),
               k.dma(SPQ, crow[0:1, 1024:2048], m_b2e1, deps=gdep),
               k.dma(SPQ, sb2row, m_sb2, deps=gdep)]
        tc1 = k.op("dve", lambda p: p.tensor_tensor(crow[0:1, 0:1024], crow[0:1, 0:1024],
                                                    crow[0:1, 1024:2048], ALU.add), deps=tb2, tok=True)
        tc2 = k.op("dve", lambda p: p.tensor_tensor(crow[0:1, 0:1024], crow[0:1, 0:1024], sb2row,
                                                    ALU.add), deps=[tc1], tok=True)
        tc3 = k.op("dve", lambda p: p.tensor_tensor(crow[0:1, 0:1024], crow[0:1, 0:1024],
                                                    ksb[0:1, 0:1024], ALU.add),
                   deps=[tc2, ksb_tok], tok=True)
        tc4 = k.op("dve", lambda p: p.tensor_tensor(crow[0:1, 1024:2048], ksb[0:1, 1024:2048],
                                                    ksb[0:1, 0:1024], ALU.subtract),
                   deps=[tc3], tok=True)
        tw1 = k.dma(SPQ, sp_k.ap(), crow, deps=[tc4])
        tw2a = k.dma(SPQ, kcols[:, 0:8],
                     sp_k.ap()[0][0:1024].rearrange("(a p) -> p a", p=128), deps=[tw1], slow=True)
        tw2 = k.dma(SPQ, kcols[:, 8:16],
                     sp_k.ap()[0][1024:2048].rearrange("(a p) -> p a", p=128), deps=[tw2a], slow=True)
        kc_toks = []
        for dmt in range(8):
            t1 = k.op("dve", lambda p, dmt=dmt: p.scalar_tensor_tensor(
                y_acc[:, dmt * 2048:(dmt + 1) * 2048], m0til, kcols[:, 8 + dmt:8 + dmt + 1],
                y_acc[:, dmt * 2048:(dmt + 1) * 2048], ALU.mult, ALU.add),
                deps=[tw2, yacc_last, tmb], tok=True)
            kc_toks.append(k.op("act", lambda p, dmt=dmt: p.add(
                y_acc[:, dmt * 2048:(dmt + 1) * 2048], y_acc[:, dmt * 2048:(dmt + 1) * 2048],
                kcols[:, dmt:dmt + 1]), deps=[t1], tok=True))
        out_dma = []
        xw2_free = None
        for c in range(NCH):
            tx = k.dma(SPQ, s3(xw2[:], 2),
                       r3(xr.ap()[c * T:(c + 1) * T, :]),
                       deps=[xw2_free] if xw2_free else [tw2])
            adds = []
            for dmt in range(8):
                for tt in range(2):
                    bank = 4 + ((dmt * 2 + tt) % 2)
                    mms = [lambda p, dmt=dmt, tt=tt, c=c, bank=bank:
                           p.transpose(ps[bank][:, 0:128],
                                       y_acc[:, dmt * 2048 + c * T + tt * 128: dmt * 2048 + c * T + (tt + 1) * 128],
                                       eye[:])]

                    def evt(p, dmt=dmt, tt=tt, bank=bank):
                        return p.tensor_tensor(
                            xw2[:, tt * 1024 + dmt * 128: tt * 1024 + (dmt + 1) * 128],
                            xw2[:, tt * 1024 + dmt * 128: tt * 1024 + (dmt + 1) * 128],
                            ps[bank][:, 0:128], ALU.add)
                    adds.append(k.mm_group(f"b{bank}", mms, "dve", evt,
                                           extra_deps=kc_toks + [tx, t_eye]))
            tout = k.dma(SPQ, r3(out_t[c * T:(c + 1) * T, :]),
                         s3(xw2[:], 2), deps=adds)
            out_dma.append(tout)
            xw2_free = tout
        def final_wait(p, toks=tuple(out_dma)):
            for sem, val in toks:
                p.wait_ge(k.h[sem], val)
            return p.engine_nop() if False else p.nop()
        k.op("pool", lambda p: p.nop(), deps=out_dma)

    STAGE = int(os.environ.get("MK_STAGE", "99"))
    tok_fold = emit_fold()
    for kk in ("zT_free", "uT_free", "dT_free", "btl_free", "ctl_free", "xn_free",
               "xnT_free", "xw_free"):
        st[kk] = tok_fold
    k.scan_guard[0] = tok_fold
    k.scan_guard[1] = tok_fold
    if STAGE >= 1:
        hin0 = mamba_phaseA(0)
    if STAGE >= 2:
        mamba_phaseB(0, hin0)
        st["hin_free"] = st["yg_tok"][NDT - 1]
    if STAGE >= 3:
        l0_ffn(tok_fold)
    if STAGE >= 4:
        hin1 = mamba_phaseA(1)
    if STAGE >= 5:
        mamba_phaseB(1, hin1)
    if STAGE >= 6:
        moe()
    if STAGE < 6:
        touts = []
        for c in range(NCH):
            touts.append(k.dma(SPQ, out_t[c * T:(c + 1) * T, :], xr.ap()[c * T:(c + 1) * T, :],
                               deps=[st["xr_tok"][c]] if st["xr_tok"][c] else []))
        k.op("pool", lambda p: p.nop(), deps=touts)

    import contextlib
    with contextlib.ExitStack() as stack:
        for sn in k.semnames:
            k.h[sn] = stack.enter_context(nc.semaphore("s_" + sn))
        block = stack.enter_context(nc.Block())

        @block.tensor
        def _(p):
            for fn in k.q["pe"]:
                fn(p)

        @block.scalar
        def _(p):
            for fn in k.q["act"]:
                fn(p)

        @block.vector
        def _(p):
            for fn in k.q["dve"]:
                fn(p)

        @block.gpsimd
        def _(p):
            for fn in k.q["pool"]:
                fn(p)

        @block.sync
        def _(p):
            for fn in k.q["sp"]:
                fn(p)

    return nc


_NC_CACHE = [None]


def kernel(**inputs):
    f32 = np.float32

    def c(a):
        return np.ascontiguousarray(a, dtype=f32)

    x = c(inputs["L0_attn_x"]).reshape(16384, 1024)
    D = 1024
    jj = np.arange(D)
    ang = 2.0 * np.pi / D * np.outer(jj, jj)
    Fr = np.cos(ang).astype(f32)
    Fi = (-np.sin(ang)).astype(f32)
    Grd = (np.cos(ang) / D).astype(f32)
    Gnid = (-np.sin(ang) / D).astype(f32)
    eye = np.eye(128, dtype=f32)

    base = {
        "w0_in_t": c(inputs["L0_attn_W_in"].T),
        "w0_conv": c(inputs["L0_attn_conv_w"].reshape(2048, 4)),
        "w0_convb": c(inputs["L0_attn_conv_b"].reshape(2048, 1)),
        "w0_x_t": c(inputs["L0_attn_W_x"].T),
        "w0_dt_t": c(inputs["L0_attn_W_dt"].T),
        "w0_dtb": c(inputs["L0_attn_b_dt"].reshape(2048, 1)),
        "w0_alog": c(inputs["L0_attn_A_log"]),
        "w0_dp": c(inputs["L0_attn_D_param"].reshape(2048, 1)),
        "w0_out": c(inputs["L0_attn_W_out"].T),
        "f0_wr": c(inputs["L0_ffn_w_freq_real"]),
        "f0_wi": c(inputs["L0_ffn_w_freq_imag"]),
        "f0_b": c(inputs["L0_ffn_bias"].reshape(1, 1024)),
        "w1_normw": c(inputs["L1_attn_norm_w"].reshape(1, 1024)),
        "w1_in": c(inputs["L1_attn_W_in"]),
        "w1_conv": c(inputs["L1_attn_conv_w"].reshape(2048, 4)),
        "w1_convb": c(inputs["L1_attn_conv_b"].reshape(2048, 1)),
        "w1_x": c(inputs["L1_attn_W_x"]),
        "w1_wdt": c(inputs["L1_attn_w_dt"].reshape(2048, 1)),
        "w1_dtb": c(inputs["L1_attn_b_dt"].reshape(2048, 1)),
        "w1_alog": c(inputs["L1_attn_A_log"]),
        "w1_dp": c(inputs["L1_attn_D_param"].reshape(2048, 1)),
        "w1_out": c(inputs["L1_attn_W_out"]),
        "m_gw_t": c(inputs["L1_ffn_gate_w"].T),
        "m_w1e0": c(inputs["L1_ffn_w1"][0]),
        "m_w1e1": c(inputs["L1_ffn_w1"][1]),
        "m_sw1": c(inputs["L1_ffn_sw1"]),
        "m_b1e0": c(inputs["L1_ffn_b1"][0].reshape(1, 4096)),
        "m_b1e1": c(inputs["L1_ffn_b1"][1].reshape(1, 4096)),
        "m_sb1": c(inputs["L1_ffn_sb1"].reshape(1, 4096)),
        "m_w2e0": c(inputs["L1_ffn_w2"][0]),
        "m_w2e1": c(inputs["L1_ffn_w2"][1]),
        "m_sw2": c(inputs["L1_ffn_sw2"]),
        "m_b2e0": c(inputs["L1_ffn_b2"][0].reshape(1, 1024)),
        "m_b2e1": c(inputs["L1_ffn_b2"][1].reshape(1, 1024)),
        "m_sb2": c(inputs["L1_ffn_sb2"].reshape(1, 1024)),
        "c_eye": eye,
        "c_ones": np.ones((1, 128), f32),
        "c_zeros": np.zeros((128, 256), f32),
        "c_cst": np.tile(np.array([1e-5, 1e-6, -0.5, 1.0 / 1024.0, 0.0, -1.0, 1.0, 0.0], f32),
                         (128, 1)),
        "c_grd": Grd,
        "c_gnid": Gnid,
    }

    if _NC_CACHE[0] is None:
        _NC_CACHE[0] = build_program()
    nc = _NC_CACHE[0]

    in_maps = []
    for i in range(NC):
        d = dict(base)
        xs = np.zeros((L + 3, DM), f32)
        xs[3:] = x[i * L:(i + 1) * L]
        if i > 0:
            xs[:3] = x[i * L - 3:i * L]
        d["x"] = xs
        d["c_frs"] = c(Fr[:, i * 128:(i + 1) * 128])
        d["c_fips"] = c(Fi[:, i * 128:(i + 1) * 128])
        d["c_fins"] = c(-Fi[:, i * 128:(i + 1) * 128])
        chm = np.zeros((128, 8), f32)
        chm[:, :i] = 1.0
        d["c_chm"] = chm
        sel3 = np.zeros((24, 3), f32)
        if i > 0:
            for r in range(3):
                sel3[3 * (i - 1) + r, r] = 1.0
        d["c_sel3"] = sel3
        in_maps.append(d)

    res = bass_utils.run_bass_kernel_spmd(nc, in_maps, core_ids=list(range(NC)))
    out = np.concatenate([np.asarray(res.results[i]["out"]) for i in range(NC)], axis=0)
    return out.reshape(1, 16384, 1024).astype(np.float32)


if __name__ == "__main__":
    build_program()
    print("build ok")


# version 02
# speedup vs baseline: 1.0080x; 1.0080x over previous
import os
import numpy as np
import concourse.bass as bass
import concourse.mybir as mybir
import concourse.bass_utils as bass_utils

F32 = mybir.dt.float32
AF = mybir.ActivationFunctionType
ALU = mybir.AluOpType
AX = mybir.AxisListType

NC = 8
L = 2048
DM = 1024
DI = 2048
N = 16
T = 256
NCH = L // T
NDT = DI // 128
TS = 128
HB = 256
NHB = 4096 // HB

ENGS = ("pe", "act", "dve", "pool", "sp")


def r3(dram_ap, p=128):
    return dram_ap.rearrange("(a p) c -> p a c", p=p)


def s3(sbuf_ap, a):
    return sbuf_ap.rearrange("p (a c) -> p a c", a=a)


NDMASEM = 12


class KB:
    def __init__(self, nc):
        self.nc = nc
        self.q = {e: [] for e in ENGS}
        self.semnames = (["pe", "act", "dve", "pool"] +
                         [f"spq{i}" for i in range(NDMASEM)] +
                         [f"poolq{i}" for i in range(NDMASEM)])
        self.cnt = {s: 0 for s in self.semnames}
        self.h = {}
        self.lastwait = {e: {} for e in ENGS}
        self.bank_tok = {}
        self.free = {}
        self.scan_guard = {}
        self.rr = {"spq": 0, "poolq": 0}

    def _wait(self, eng, tok):
        if tok is None:
            return
        sem, val = tok
        lw = self.lastwait[eng].get(sem, 0)
        if val <= lw:
            return
        self.lastwait[eng][sem] = val

        def f(p, s=sem, v=val):
            p.wait_ge(self.h[s], v)
        self.q[eng].append(f)

    def op(self, eng, fn, deps=(), tok=False):
        for d in deps:
            self._wait(eng, d)
        if tok:
            self.cnt[eng] += 1

            def f(p, fn=fn, e=eng):
                fn(p).then_inc(self.h[e], 1)
            self.q[eng].append(f)
            return (eng, self.cnt[eng])

        def f(p, fn=fn):
            fn(p)
        self.q[eng].append(f)
        return None

    def dma(self, qn, out, in_, deps=(), slow=False):
        eng = "sp" if qn == "spq" else "pool"
        sem = f"{qn}{self.rr[qn]}"
        self.rr[qn] = (self.rr[qn] + 1) % NDMASEM
        self._wait(eng, (sem, self.cnt[sem]))
        for d in deps:
            self._wait(eng, d)
        self.cnt[sem] += 16

        def f(p, o=out, i=in_, s=sem, slow=slow):
            if slow:
                p.dma_start(out=o, in_=i, allow_slow_non_contiguous=True).then_inc(self.h[s], 16)
            else:
                p.dma_start(out=o, in_=i).then_inc(self.h[s], 16)
        self.q[eng].append(f)
        return (sem, self.cnt[sem])

    def mm_group(self, bank, mms, evac_eng, evac_fn, extra_deps=()):
        dep = self.bank_tok.get(bank)
        deps0 = [d for d in extra_deps if d is not None] + ([dep] if dep else [])
        for i, m in enumerate(mms[:-1]):
            self.op("pe", m, deps=deps0 if i == 0 else ())
        pet = self.op("pe", mms[-1], deps=deps0 if len(mms) == 1 else (), tok=True)
        ct = self.op(evac_eng, evac_fn, deps=[pet], tok=True)
        self.bank_tok[bank] = ct
        return ct


def build_program():
    nc = bass.Bass("TRN2", target_bir_lowering=False, debug=False)
    k = KB(nc)

    def din(name, shape):
        return nc.dram_tensor(name, list(shape), F32, kind="ExternalInput").ap()

    x_in = din("x", (L + 3, DM))
    w_in_t = [din("w0_in_t", (DM, 2 * DI)), din("w1_in", (DM, 2 * DI))]
    w_conv = [din("w0_conv", (DI, 4)), din("w1_conv", (DI, 4))]
    w_convb = [din("w0_convb", (DI, 1)), din("w1_convb", (DI, 1))]
    w_x_t = [din("w0_x_t", (DI, 96)), din("w1_x", (DI, 33))]
    w0_dt_t = din("w0_dt_t", (64, DI))
    w_dtb = [din("w0_dtb", (DI, 1)), din("w1_dtb", (DI, 1))]
    w_alog = [din("w0_alog", (DI, N)), din("w1_alog", (DI, N))]
    w_dp = [din("w0_dp", (DI, 1)), din("w1_dp", (DI, 1))]
    w_out = [din("w0_out", (DI, DM)), din("w1_out", (DI, DM))]
    w1_wdt = din("w1_wdt", (DI, 1))
    w1_normw = din("w1_normw", (1, DM))
    f0_wr = din("f0_wr", (DM, DM))
    f0_wi = din("f0_wi", (DM, DM))
    f0_b = din("f0_b", (1, DM))
    m_gw_t = din("m_gw_t", (DM, 2))
    m_w1 = [din("m_w1e0", (DM, 4096)), din("m_w1e1", (DM, 4096)), din("m_sw1", (DM, 4096))]
    m_b1 = [din("m_b1e0", (1, 4096)), din("m_b1e1", (1, 4096)), din("m_sb1", (1, 4096))]
    m_w2 = [din("m_w2e0", (4096, DM)), din("m_w2e1", (4096, DM)), din("m_sw2", (4096, DM))]
    m_b2e0 = din("m_b2e0", (1, DM))
    m_b2e1 = din("m_b2e1", (1, DM))
    m_sb2 = din("m_sb2", (1, DM))
    c_eye = din("c_eye", (128, 128))
    c_grd = din("c_grd", (DM, DM))
    c_gnid = din("c_gnid", (DM, DM))
    c_frs = din("c_frs", (DM, 128))
    c_fips = din("c_fips", (DM, 128))
    c_fins = din("c_fins", (DM, 128))
    c_chm = din("c_chm", (128, 8))
    c_cst = din("c_cst", (128, 8))
    c_ones = din("c_ones", (1, 128))
    c_zeros = din("c_zeros", (128, 256))
    c_sel3 = din("c_sel3", (24, 3))

    out_t = nc.dram_tensor("out", [L, DM], F32, kind="ExternalOutput").ap()

    def dsc(name, shape):
        return nc.dram_tensor(name, list(shape), F32, kind="Internal")

    xr = dsc("xr", (L, DM))
    sp_u = dsc("sp_u", (DI, L))
    sp_d = dsc("sp_d", (DI, L))
    sp_z = dsc("sp_z", (DI, L))
    sp_bc = [dsc("sp_bc0", (33, L)), dsc("sp_bc1", (33, L))]
    sp_xnt = dsc("sp_xnt", (DM, L))
    sp_m0 = dsc("sp_m0", (1, L))
    sp_k = dsc("sp_k", (1, 2 * DM))
    ag0_in = dsc("ag0_in", (128, DM))
    ag0_out = dsc("ag0_out", (DM, DM))
    ag_in = [dsc("ag1_in", (128, 512)), dsc("ag3_in", (128, 512))]
    ag_out = [dsc("ag1_out", (1024, 512)), dsc("ag3_out", (1024, 512))]
    ag2_in = dsc("ag2_in", (3, DM))
    ag2_out = dsc("ag2_out", (24, DM))

    POOLC = 47104
    PB = nc.alloc_sbuf_tensor("PB", [128, POOLC], F32)

    def pv(off, ncols):
        assert off % 8 == 0 and off + ncols <= POOLC, (off, ncols)
        return PB[:, off:off + ncols]

    O = {}
    _o = [0]

    def seg(name, n):
        O[name] = _o[0]
        _o[0] += n
        return pv(O[name], n)

    xwin = seg("xwin", 2048)
    uT = seg("uT", 4096)
    dT = seg("dT", 4096)
    zT = seg("zT", 4096)
    Btl = seg("Btl", 4096)
    Ctl = seg("Ctl", 4096)
    abuf = [seg("a0", 2048), seg("a1", 2048)]
    bbuf = [seg("b0", 2048), seg("b1", 2048)]
    hbuf = [seg("h0", 2048), seg("h1", 2048)]
    PLC = seg("PLC", 8192)
    wb = [seg("wb0", 2048), seg("wb1", 2048)]
    assert _o[0] <= POOLC, _o[0]

    xn_v = PLC[:, 0:1024]
    xnT_v = PLC[:, 1024:3072]
    xpT_v = PLC[:, 3072:3072 + 16 * (T + 3)]
    du_v = PLC[:, 0:4096]
    ygT_v = PLC[:, 4096:8192]
    agsb = PLC[:, 4096:8192]
    F_PT = uT
    F_QT = dT
    F_FR = zT[:, 0:1024]
    F_FI2 = Btl[:, 0:2048]
    F_WS = [Ctl[:, 0:2048], Ctl[:, 2048:4096]]
    F_GM = [hbuf[0][:, 0:512], hbuf[0][:, 512:1024], hbuf[1][:, 0:512], hbuf[1][:, 512:1024]]
    F_M = PLC[:, 0:1024]
    Mres = [Btl, Ctl]
    MO = {}
    _m = [0]

    def mseg(name, n):
        MO[name] = _m[0]
        _m[0] += n
        return pv(MO[name], n)

    y_acc = mseg("y_acc", 16384)
    hbv = [mseg("hb0", 1024), mseg("hb1", 1024), mseg("hbs", 1024)]
    w1s = [mseg("w1s0", 2048), mseg("w1s1", 2048), mseg("w1s2", 2048)]
    w2s = [mseg("w2s0", 2048), mseg("w2s1", 2048), mseg("w2s2", 2048)]
    xnw = mseg("xnw", 4096)
    m0til = mseg("m0til", 2048)
    xw2 = mseg("xw2", 2048)
    xn2 = mseg("xn2", 1024)
    xnT2 = mseg("xnT2", 2048)
    ksb_v = mseg("ksb", 1024)
    b1r_v = mseg("b1r", 1024)
    assert _m[0] <= POOLC, _m[0]

    eye = nc.alloc_sbuf_tensor("eye", [128, 128], F32)
    A_sb = [nc.alloc_sbuf_tensor("A0", [128, 256], F32),
            nc.alloc_sbuf_tensor("A1", [128, 256], F32)]
    cc = [nc.alloc_sbuf_tensor("cc0", [128, 128], F32),
          nc.alloc_sbuf_tensor("cc1", [128, 128], F32)]
    lncol = nc.alloc_sbuf_tensor("lncol", [128, 16], F32)
    xdbl = nc.alloc_sbuf_tensor("xdbl", [128, 256], F32)
    Stot = nc.alloc_sbuf_tensor("Stot", [128, 16], F32)
    Dtot = nc.alloc_sbuf_tensor("Dtot", [128, 256], F32)
    carry = nc.alloc_sbuf_tensor("carry", [128, 256], F32)
    hin = nc.alloc_sbuf_tensor("hin", [128, 256], F32)
    chm = nc.alloc_sbuf_tensor("chm", [128, 8], F32)
    onesr = nc.alloc_sbuf_tensor("onesr", [1, 128], F32)
    m0buf = nc.alloc_sbuf_tensor("m0buf", [128, 16], F32)
    ycol = nc.alloc_sbuf_tensor("ycol", [128, 256], F32)
    tcol = nc.alloc_sbuf_tensor("tcol", [128, 256], F32)
    cst = nc.alloc_sbuf_tensor("cst", [128, 8], F32)
    dtrawb = ycol[:, 0:256]
    gcol = xdbl[:, 0:256]
    hxs = nc.alloc_sbuf_tensor("hxs", [128, 1024], F32)
    sel3 = nc.alloc_sbuf_tensor("sel3", [24, 3], F32)
    xnTh = nc.alloc_sbuf_tensor("xnTh", [128, 24], F32)
    kbuf = nc.alloc_sbuf_tensor("kbuf", [128, 96], F32)
    b1col = nc.alloc_sbuf_tensor("b1col", [128, 96], F32)
    kcols = nc.alloc_sbuf_tensor("kcols", [128, 16], F32)
    gwsb = nc.alloc_sbuf_tensor("gwsb", [128, 16], F32)
    normw = nc.alloc_sbuf_tensor("normw", [128, 1024], F32)
    f0b = nc.alloc_sbuf_tensor("f0b", [1, 1024], F32)
    ksb = PB[0:1, MO["xw2"]:MO["xw2"] + 2048]
    crow = PB[0:1, MO["xnT2"]:MO["xnT2"] + 2048]
    sb2row = PB[0:1, MO["xn2"]:MO["xn2"] + 1024]

    ps = [nc.alloc_psum_tensor(f"ps{i}", [128, 512], F32) for i in range(8)]

    SPQ, PLQ = "spq", "poolq"
    RG = [list(range(NC))]

    st = {
        "wb_tok": [None, None],
        "xp_evac": [None] * NDT,
        "xp_tail": [None] * NDT,
        "btl_free": None, "ctl_free": None,
        "xr_tok": [None] * NCH,
        "u_sp": [[None] * NCH, [None] * NCH],
        "d_sp": [[None] * NCH, [None] * NCH],
        "z_sp": [[None] * NCH, [None] * NCH],
        "yg_tok": [None] * NDT,
        "carry_tok": [None] * NDT,
        "wo_tws": [None] * NDT,
        "xw_free": None, "uT_free": None, "dT_free": None, "zT_free": None,
        "xdbl_free": None, "tspd": None, "du_free": None, "bc_sp": [],
        "stot_last": None,
        "ygT_free": None, "gcol_free": None, "ag2src": None,
        "xn_free": None, "xnT_free": None,
    }

    t_eye = k.dma(SPQ, eye[:], c_eye)
    t_chm = k.dma(SPQ, chm[:], c_chm)
    t_sel3 = k.dma(SPQ, sel3[:], c_sel3)
    t_A = [None, None]
    t_cc = [None, None]
    for lay in range(2):
        csb = cc[lay]
        k.dma(SPQ, s3(csb[:, 0:64], 16), r3(w_conv[lay]))
        k.dma(SPQ, s3(csb[:, 64:80], 16), r3(w_convb[lay]), slow=True)
        k.dma(SPQ, s3(csb[:, 80:96], 16), r3(w_dp[lay]), slow=True)
        t_cc[lay] = k.dma(SPQ, s3(csb[:, 96:112], 16), r3(w_dtb[lay]), slow=True)
    t_wdt1 = k.dma(SPQ, s3(cc[1][:, 112:128], 16), r3(w1_wdt), slow=True)
    t_ones = k.dma(SPQ, onesr[:], c_ones)
    t_cst = k.dma(SPQ, cst[:], c_cst)
    for lay in range(2):
        tl = k.dma(SPQ, s3(A_sb[lay][:], 16), r3(w_alog[lay]))
        ta = k.op("act", lambda p, s=A_sb[lay]: p.activation(out=s[:], in_=s[:], func=AF.Exp),
                  deps=[tl], tok=True)
        t_A[lay] = k.op("dve", lambda p, s=A_sb[lay]: p.tensor_scalar_mul(s[:], s[:], cst[:, 5:6]),
                        deps=[ta, t_cst], tok=True)

    def emit_fold():
        tFR = k.dma(SPQ, s3(F_FR, 8), r3(c_frs))
        tFI = [k.dma(SPQ, s3(F_FI2[:, 0:1024], 8), r3(c_fins)),
               k.dma(SPQ, s3(F_FI2[:, 1024:2048], 8), r3(c_fips))]
        prev_grp = [None, None]
        for which, dstv in enumerate((F_PT, F_QT)):
            stat1 = f0_wr if which == 0 else f0_wi
            stat2 = f0_wi if which == 0 else f0_wr
            mv2 = F_FI2[:, 0:1024] if which == 0 else F_FI2[:, 1024:2048]
            for ot in range(8):
                par = ot % 2
                tw1 = k.dma(SPQ, s3(F_WS[par][:, 0:1024], 8),
                            r3(stat1[:, ot * 128:(ot + 1) * 128]),
                            deps=[prev_grp[par]] if prev_grp[par] else [])
                tw2 = k.dma(SPQ, s3(F_WS[par][:, 1024:2048], 8),
                            r3(stat2[:, ot * 128:(ot + 1) * 128]))
                bank = 4 + par
                mms = []
                for kt in range(8):
                    mms.append(lambda p, kt=kt, par=par, bank=bank:
                               p.matmul(ps[bank][:, 0:128], F_WS[par][:, kt * 128:(kt + 1) * 128],
                                        F_FR[:, kt * 128:(kt + 1) * 128], start=(kt == 0), stop=False))
                for kt in range(8):
                    mms.append(lambda p, kt=kt, par=par, bank=bank, mv2=mv2:
                               p.matmul(ps[bank][:, 0:128], F_WS[par][:, 1024 + kt * 128:1024 + (kt + 1) * 128],
                                        mv2[:, kt * 128:(kt + 1) * 128], start=False, stop=(kt == 7)))

                def evac(p, ot=ot, dstv=dstv, bank=bank):
                    return p.copy(dstv[:, ot * 128:(ot + 1) * 128], ps[bank][:, 0:128])
                prev_grp[par] = k.mm_group(f"b{bank}", mms, "act", evac,
                                           extra_deps=[tw1, tw2, tFR, tFI[which]])
        GSL = [abuf[0][:, 0:512], abuf[0][:, 512:1024], abuf[1][:, 0:512], abuf[1][:, 512:1024],
               bbuf[0][:, 0:512], bbuf[0][:, 512:1024], bbuf[1][:, 0:512], bbuf[1][:, 512:1024],
               hbuf[0][:, 0:512], hbuf[0][:, 512:1024], hbuf[1][:, 0:512], hbuf[1][:, 512:1024],
               Btl[:, 2048:2560], Btl[:, 2560:3072], Btl[:, 3072:3584], Btl[:, 3584:4096]]
        tok_m = []
        gmdep = None
        for half in range(2):
            bank = 6
            deps = []
            for ot in range(8):
                deps.append(k.dma(SPQ, GSL[ot],
                                  c_grd[ot * 128:(ot + 1) * 128, half * 512:(half + 1) * 512],
                                  deps=[gmdep] if gmdep else []))
                deps.append(k.dma(SPQ, GSL[8 + ot],
                                  c_gnid[ot * 128:(ot + 1) * 128, half * 512:(half + 1) * 512],
                                  deps=[gmdep] if gmdep else []))
            mms = []
            for ot in range(8):
                mms.append(lambda p, ot=ot, bank=bank:
                           p.matmul(ps[bank][:], F_PT[:, ot * 128:(ot + 1) * 128], GSL[ot],
                                    start=(ot == 0), stop=False))
                mms.append(lambda p, ot=ot, bank=bank:
                           p.matmul(ps[bank][:], F_QT[:, ot * 128:(ot + 1) * 128], GSL[8 + ot],
                                    start=False, stop=(ot == 7)))

            def evac(p, half=half, bank=bank):
                return p.copy(F_M[:, half * 512:(half + 1) * 512], ps[bank][:])
            tok_m.append(k.mm_group(f"b{bank}", mms, "act", evac,
                                    extra_deps=deps + [prev_grp[0], prev_grp[1]]))
            gmdep = tok_m[-1]
        tw = k.dma(SPQ, ag0_in.ap(), F_M, deps=tok_m)
        tag = k.op("pool", lambda p: p.collective_compute(
            "AllGather", ALU.bypass, ins=[ag0_in.ap()], outs=[ag0_out.ap()],
            replica_groups=RG), deps=[tw], tok=True)
        return tag

    def ln_tile(src_ap, dst_ap, sl, eps, deps=()):
        c = lncol[:, 4 * sl:4 * sl + 4]
        dl = [d for d in deps if d is not None]
        t1 = k.op("dve", lambda p: p.reduce_sum(c[:, 0:1], src_ap, axis=AX.X), deps=dl, tok=True)
        t2 = k.op("act", lambda p: p.activation(out=zT[:, 0:1024], in_=src_ap, func=AF.Square,
                                                accum_out=c[:, 1:2]),
                  deps=dl + [st["zT_free"]], tok=True)
        k.op("act", lambda p: p.mul(c[:, 0:1], c[:, 0:1], cst[:, 3:4]), deps=[t1, t_cst])
        t3 = k.op("act", lambda p: p.mul(c[:, 1:2], c[:, 1:2], cst[:, 3:4]), deps=[t2], tok=True)
        t4 = k.op("dve", lambda p: p.tensor_tensor(c[:, 2:3], c[:, 0:1], c[:, 0:1], ALU.mult),
                  deps=[t3], tok=True)
        t5 = k.op("dve", lambda p: p.tensor_tensor(c[:, 2:3], c[:, 1:2], c[:, 2:3], ALU.subtract),
                  deps=[t4], tok=True)
        t6 = k.op("act", lambda p: p.activation(out=c[:, 2:3], in_=c[:, 2:3], func=AF.Ln,
                                                bias=cst[:, 0:1]), deps=[t5], tok=True)
        t7 = k.op("act", lambda p: p.activation(out=c[:, 2:3], in_=c[:, 2:3], func=AF.Exp,
                                                scale=cst[:, 2:3]), deps=[t6], tok=True)
        return k.op("dve", lambda p: p.tensor_scalar(dst_ap, src_ap, c[:, 0:1], c[:, 2:3],
                                                     ALU.subtract, ALU.mult), deps=[t7], tok=True)

    def rms_tile(xn_ap, sl, dep, nw_ap, nw_dep):
        c = lncol[:, 4 * sl + 3:4 * sl + 4]
        t2 = k.op("act", lambda p: p.activation(out=zT[:, 1024:2048], in_=xn_ap, func=AF.Square,
                                                accum_out=c), deps=[dep, st["zT_free"]], tok=True)
        t6 = k.op("act", lambda p: p.activation(out=c, in_=c, func=AF.Ln,
                                                scale=cst[:, 3:4], bias=cst[:, 1:2]), deps=[t2], tok=True)
        t7 = k.op("act", lambda p: p.activation(out=c, in_=c, func=AF.Exp, scale=cst[:, 2:3]),
                  deps=[t6], tok=True)
        t8 = k.op("dve", lambda p: p.tensor_scalar_mul(xn_ap, xn_ap, c), deps=[t7], tok=True)
        return k.op("dve", lambda p: p.tensor_tensor(xn_ap, xn_ap, nw_ap, ALU.mult),
                    deps=[t8, nw_dep], tok=True)

    def transpose_tile(src_ap, dst_ap, bank, deps):
        mms = [lambda p: p.transpose(ps[bank][:, 0:128], src_ap, eye[:])]

        def evac(p):
            return p.copy(dst_ap, ps[bank][:, 0:128])
        return k.mm_group(f"b{bank}", mms, "act", evac, extra_deps=list(deps) + [t_eye])

    def halo_xp(lay, dep_src):
        win = w_in_t[lay]
        c = lncol[0:3, 0:4]
        dl = [dep_src]
        t1 = k.op("dve", lambda p: p.reduce_sum(c[:, 0:1], hxs[0:3, :], axis=AX.X), deps=dl, tok=True)
        t2 = k.op("act", lambda p: p.activation(out=zT[0:3, 0:1024], in_=hxs[0:3, :], func=AF.Square,
                                                accum_out=c[:, 1:2]), deps=dl + [st["zT_free"]], tok=True)
        k.op("act", lambda p: p.mul(c[:, 0:1], c[:, 0:1], cst[0:3, 3:4]), deps=[t1, t_cst])
        t3 = k.op("act", lambda p: p.mul(c[:, 1:2], c[:, 1:2], cst[0:3, 3:4]), deps=[t2], tok=True)
        t4 = k.op("dve", lambda p: p.tensor_tensor(c[:, 2:3], c[:, 0:1], c[:, 0:1], ALU.mult),
                  deps=[t3], tok=True)
        t5 = k.op("dve", lambda p: p.tensor_tensor(c[:, 2:3], c[:, 1:2], c[:, 2:3], ALU.subtract),
                  deps=[t4], tok=True)
        t6 = k.op("act", lambda p: p.activation(out=c[:, 2:3], in_=c[:, 2:3], func=AF.Ln,
                                                bias=cst[0:3, 0:1]), deps=[t5], tok=True)
        t7 = k.op("act", lambda p: p.activation(out=c[:, 2:3], in_=c[:, 2:3], func=AF.Exp,
                                                scale=cst[0:3, 2:3]), deps=[t6], tok=True)
        tn = k.op("dve", lambda p: p.tensor_scalar(hxs[0:3, :], hxs[0:3, :], c[:, 0:1], c[:, 2:3],
                                                   ALU.subtract, ALU.mult), deps=[t7], tok=True)
        if lay == 1:
            t2b = k.op("act", lambda p: p.activation(out=zT[0:3, 1024:2048], in_=hxs[0:3, :], func=AF.Square,
                                                     accum_out=c[:, 3:4]), deps=[tn], tok=True)
            t6b = k.op("act", lambda p: p.activation(out=c[:, 3:4], in_=c[:, 3:4], func=AF.Ln,
                                                     scale=cst[0:3, 3:4], bias=cst[0:3, 1:2]), deps=[t2b], tok=True)
            t7b = k.op("act", lambda p: p.activation(out=c[:, 3:4], in_=c[:, 3:4], func=AF.Exp,
                                                     scale=cst[0:3, 2:3]), deps=[t6b], tok=True)
            tn2 = k.op("dve", lambda p: p.tensor_scalar_mul(hxs[0:3, :], hxs[0:3, :], c[:, 3:4]),
                       deps=[t7b], tok=True)
            tn = k.op("dve", lambda p: p.tensor_tensor(hxs[0:3, :], hxs[0:3, :], normw[0:3, :],
                                                       ALU.mult), deps=[tn2, st["nw_tok"]], tok=True)
        tlast = None
        for dmt in range(8):
            bank = 5
            mms = [lambda p, dmt=dmt: p.transpose(ps[bank][0:128, dmt * 4:dmt * 4 + 3],
                                                  hxs[0:3, dmt * 128:(dmt + 1) * 128],
                                                  eye[0:3, 0:3])]

            def evac(p, dmt=dmt):
                return p.copy(xnTh[:, dmt * 3:(dmt + 1) * 3], ps[bank][:, dmt * 4:dmt * 4 + 3])
            tlast = k.mm_group(f"b{bank}", mms, "act", evac, extra_deps=[tn, t_eye])
        for dt in range(NDT):
            par = dt % 2
            tws = k.dma(SPQ, s3(wb[par][:, 0:1024], 8),
                        r3(win[:, dt * 128:(dt + 1) * 128]),
                        deps=[st["wb_tok"][par]])
            bank = 6
            mms = []
            for dmt in range(8):
                mms.append(lambda p, dmt=dmt, par=par, bank=bank:
                           p.matmul(ps[bank][:, 0:3], wb[par][:, dmt * 128:(dmt + 1) * 128],
                                    xnTh[:, dmt * 3:(dmt + 1) * 3], start=(dmt == 0), stop=(dmt == 7)))

            def evac(p, dt=dt, bank=bank):
                return p.copy(xpT_v[:, dt * (T + 3):dt * (T + 3) + 3], ps[bank][:, 0:3])
            ct = k.mm_group(f"b{bank}", mms, "act", evac, extra_deps=[tws, tlast])
            st["wb_tok"][par] = ct
            st["xp_tail"][dt] = ct

    def emit_scan(lay, c, phaseB, d_dep, du_toks, bt_toks, ct_toks=None, hin_tok=None, lite=False):
        asb = A_sb[lay]
        csb = cc[lay]
        src_du = du_v if phaseB else uT
        dve_last = None
        for dt in range(NDT):
            for s in range(2):
                par = (dt * 2 + s) % 2
                guard = k.scan_guard.get(par)
                a_toks = []
                for n in range(N):
                    a_toks.append(k.op("act", lambda p, dt=dt, s=s, n=n, par=par: p.activation(
                        out=abuf[par][:, n * TS:(n + 1) * TS],
                        in_=dT[:, dt * T + s * TS: dt * T + (s + 1) * TS],
                        func=AF.Exp, scale=asb[:, dt * 16 + n:dt * 16 + n + 1]),
                        deps=[d_dep[dt], t_A[lay], guard] if n == 0 else [], tok=(n == N - 1)))
                b_toks = []
                for n in range(N):
                    b_toks.append(k.op("pool", lambda p, dt=dt, s=s, n=n, par=par: p.tensor_tensor(
                        bbuf[par][:, n * TS:(n + 1) * TS],
                        src_du[:, dt * T + s * TS: dt * T + (s + 1) * TS],
                        Btl[:, n * T + s * TS: n * T + s * TS + TS], ALU.mult),
                        deps=([du_toks[dt]] + bt_toks + [guard]) if n == 0 else [], tok=(n == N - 1)))
                sc_toks = []
                for n in range(N):
                    if c == 0 and s == 0 and dt is not None and not phaseB:
                        init = 0.0
                    elif c == 0 and s == 0:
                        init = hin[:, dt * 16 + n:dt * 16 + n + 1]
                    elif s == 0:
                        init = carry[:, dt * 16 + n:dt * 16 + n + 1]
                    else:
                        init = hbuf[1 - par][:, n * TS + TS - 1:n * TS + TS]
                    dps = [a_toks[-1], b_toks[-1]] if n == 0 else []
                    if n == 0 and s == 0:
                        dps.append(st["carry_tok"][dt])
                    if n == 0 and phaseB and c == 0 and s == 0:
                        dps.append(hin_tok)
                    sc_toks.append(k.op("dve", lambda p, n=n, par=par, init=init: p.tensor_tensor_scan(
                        hbuf[par][:, n * TS:(n + 1) * TS],
                        abuf[par][:, n * TS:(n + 1) * TS],
                        bbuf[par][:, n * TS:(n + 1) * TS],
                        init, ALU.mult, ALU.add), deps=dps, tok=(n == N - 1)))
                k.scan_guard[par] = sc_toks[-1]
                if s == 1:
                    src = hbuf[par][:].rearrange("p (n t) -> p n t", n=N)[:, :, TS - 1:TS].squeeze(2)
                    st["carry_tok"][dt] = k.op("dve", lambda p, dt=dt, src=src: p.tensor_copy(
                        carry[:, dt * 16:(dt + 1) * 16], src), deps=[sc_toks[-1]], tok=True)
                if phaseB and not lite:
                    h3 = hbuf[par][:].rearrange("p (n t) -> p n t", n=N)
                    a3 = abuf[par][:].rearrange("p (n t) -> p n t", n=N)
                    c3 = Ctl[:].rearrange("p (n t) -> p n t", n=N)[:, :, s * TS:(s + 1) * TS]
                    tm = k.op("dve", lambda p, h3=h3, a3=a3, c3=c3: p.tensor_tensor(
                        a3, h3, c3, ALU.mult),
                        deps=(ct_toks or []) + [sc_toks[-1]], tok=True)
                    k.scan_guard[par] = tm
                    ty = k.op("dve", lambda p, s=s, par=par: p.tensor_reduce(
                        ycol[:, s * TS:(s + 1) * TS],
                        abuf[par][:].rearrange("p (n t) -> p t n", n=N),
                        axis=AX.X, op=ALU.add),
                        deps=[tm, st["gcol_free"] if s == 0 else None], tok=True)
                    k.scan_guard[par] = ty
                    dve_last = ty
            if phaseB and not lite:
                tg2 = k.op("dve", lambda p, dt=dt: p.scalar_tensor_tensor(
                    ycol[:], uT[:, dt * T:(dt + 1) * T], csb[:, 80 + dt:80 + dt + 1],
                    ycol[:], ALU.mult, ALU.add), deps=[dve_last, st["uT_load"], t_cc[lay]], tok=True)
                tg3 = k.op("dve", lambda p, dt=dt: p.tensor_tensor(
                    ygT_v[:, dt * T:(dt + 1) * T], ycol[:], zT[:, dt * T:(dt + 1) * T], ALU.mult),
                    deps=[st["zsilu"], tg2, st["ygT_free"]] +
                         (list(st["xp_tail"]) if dt == 0 else []), tok=True)
                st["gcol_free"] = tg3
                st["yg_tok"][dt] = tg3
        return k.scan_guard.get(0), k.scan_guard.get(1)

    def mamba_phaseA(lay):
        win = w_in_t[lay]
        wxt = w_x_t[lay]
        spbc = sp_bc[lay]
        csb = cc[lay]
        nxp = 96 if lay == 0 else 33
        st["stot_z"] = k.dma(SPQ, Stot[:], c_zeros[:, 0:16], deps=[st.get("stot_free")])
        if lay == 1:
            st["nw_tok"] = k.dma(PLQ, normw[:], w1_normw.partition_broadcast(128),
                                 deps=[st["ctl_free"]] if False else [])
        if lay == 0:
            th = k.dma(SPQ, hxs[0:3, :], x_in[0:3, :])
        else:
            tg = k.op("pool", lambda p: p.collective_compute(
                "AllGather", ALU.bypass, ins=[ag2_in.ap()], outs=[ag2_out.ap()],
                replica_groups=RG), deps=[st["ag2src"]], tok=True)
            tl = k.dma(SPQ, hxs[0:24, :], ag2_out.ap(), deps=[tg])
            th = None
            for half in range(2):
                bank = 4
                mms = [lambda p, half=half, bank=bank:
                       p.matmul(ps[bank][0:3, :], sel3[:], hxs[0:24, half * 512:(half + 1) * 512],
                                start=True, stop=True)]

                def evac(p, half=half, bank=bank):
                    return p.copy(hxs[0:3, half * 512:(half + 1) * 512], ps[bank][0:3, :])
                th = k.mm_group(f"b{bank}", mms, "act", evac, extra_deps=[tl, t_sel3])
        halo_xp(lay, th)

        for c in range(NCH):
            if lay == 0:
                tx = k.dma(SPQ, s3(xwin[:], 2),
                           r3(x_in[3 + c * T: 3 + (c + 1) * T, :]),
                           deps=[st["xw_free"]])
            else:
                tx = k.dma(SPQ, s3(xwin[:], 2),
                           r3(xr.ap()[c * T:(c + 1) * T, :]),
                           deps=[st["xr_tok"][c], st["xw_free"]])
            ttr_toks = []
            resid_toks = []
            for tt in range(2):
                t_ln = ln_tile(xwin[:, tt * 1024:(tt + 1) * 1024], xn_v, tt, 1e-5,
                               deps=[tx, st["xn_free"]])
                if lay == 1:
                    tr = k.op("dve", lambda p, tt=tt: p.tensor_tensor(
                        xwin[:, tt * 1024:(tt + 1) * 1024], xwin[:, tt * 1024:(tt + 1) * 1024],
                        xn_v, ALU.add), deps=[t_ln], tok=True)
                    resid_toks.append(tr)
                    t_ln = rms_tile(xn_v, tt, tr, normw[:], st["nw_tok"])
                for dmt in range(8):
                    tt_tok = transpose_tile(xn_v[:, dmt * 128:(dmt + 1) * 128],
                                            xnT_v[:, dmt * 256 + tt * 128: dmt * 256 + (tt + 1) * 128],
                                            2 + dmt % 2, [t_ln, st["xnT_free"]])
                st["xn_free"] = tt_tok
                ttr_toks.append(tt_tok)
            if lay == 1:
                st["xr_tok"][c] = k.dma(SPQ, r3(xr.ap()[c * T:(c + 1) * T, :]),
                                        s3(xwin[:], 2), deps=resid_toks)
                st["xw_free"] = st["xr_tok"][c]
            else:
                st["xw_free"] = ttr_toks[-1]
            z_toks = []
            xz_last = None
            for j in range(2 * NDT):
                par = j % 2
                tws = k.dma(SPQ, s3(wb[par][:, 0:1024], 8),
                            r3(win[:, j * 128:(j + 1) * 128]),
                            deps=[st["wb_tok"][par]])
                bank = par
                mms = []
                for dmt in range(8):
                    mms.append(lambda p, dmt=dmt, par=par, bank=bank:
                               p.matmul(ps[bank][:, 0:T], wb[par][:, dmt * 128:(dmt + 1) * 128],
                                        xnT_v[:, dmt * 256:(dmt + 1) * 256],
                                        start=(dmt == 0), stop=(dmt == 7)))
                if j < NDT:
                    def evac(p, j=j, bank=bank):
                        return p.copy(xpT_v[:, j * (T + 3) + 3:(j + 1) * (T + 3)], ps[bank][:, 0:T])
                    dcopy = [st["xp_tail"][j]]
                else:
                    def evac(p, j=j, bank=bank):
                        return p.copy(zT[:, (j - NDT) * T:(j - NDT + 1) * T], ps[bank][:, 0:T])
                    dcopy = [st["zT_free"]]
                ct = k.mm_group(f"b{bank}", mms, "act", evac,
                                extra_deps=[tws] + ttr_toks + dcopy)
                st["wb_tok"][par] = ct
                xz_last = ct
                if j < NDT:
                    st["xp_evac"][j] = ct
                else:
                    z_toks.append(ct)
            st["xnT_free"] = xz_last
            tspz = k.dma(PLQ, r3(sp_z.ap()[:, c * T:(c + 1) * T]),
                         s3(zT[:], 16), deps=z_toks)
            st["z_sp"][lay][c] = tspz
            st["zT_free"] = tspz
            u_toks = []
            prev_silu = None
            for dt in range(NDT):
                base = dt * (T + 3)
                t1 = k.op("dve", lambda p, base=base, dt=dt: p.tensor_scalar_mul(
                    tcol[:], xpT_v[:, base:base + T], csb[:, dt * 4:dt * 4 + 1]),
                    deps=[st["xp_evac"][dt], t_cc[lay], prev_silu], tok=True)
                for kk in range(1, 4):
                    t1 = k.op("dve", lambda p, base=base, dt=dt, kk=kk: p.scalar_tensor_tensor(
                        tcol[:], xpT_v[:, base + kk:base + kk + T], csb[:, dt * 4 + kk:dt * 4 + kk + 1],
                        tcol[:], ALU.mult, ALU.add), deps=[t1], tok=True)
                tu = k.op("act", lambda p, dt=dt: p.activation(
                    out=uT[:, dt * T:(dt + 1) * T], in_=tcol[:], func=AF.Silu,
                    bias=csb[:, 64 + dt:64 + dt + 1]), deps=[t1, st["uT_free"]], tok=True)
                prev_silu = tu
                u_toks.append(tu)
                st["xp_tail"][dt] = k.op("dve", lambda p, base=base: p.tensor_copy(
                    xpT_v[:, base:base + 3], xpT_v[:, base + T:base + T + 3]), deps=[tu], tok=True)
            tspu = k.dma(PLQ, r3(sp_u.ap()[:, c * T:(c + 1) * T]),
                         s3(uT[:], 16), deps=u_toks)
            st["u_sp"][lay][c] = tspu
            twx = k.dma(SPQ, s3(wb[0][:, 0:16 * nxp], 16),
                        r3(wxt), deps=[st["wb_tok"][0]])
            bank = 4
            mms = []
            for dt in range(NDT):
                mms.append(lambda p, dt=dt, bank=bank, nxp=nxp:
                           p.matmul(ps[bank][0:nxp, 0:T], wb[0][:, dt * nxp:(dt + 1) * nxp],
                                    uT[:, dt * T:(dt + 1) * T], start=(dt == 0), stop=(dt == NDT - 1)))

            def evacx(p, nxp=nxp, bank=bank):
                return p.copy(xdbl[0:nxp, 0:T], ps[bank][0:nxp, 0:T])
            tx2 = k.mm_group(f"b{bank}", mms, "act", evacx,
                             extra_deps=u_toks + [twx, st["xdbl_free"]] + st["bc_sp"])
            st["wb_tok"][0] = tx2
            if lay == 0:
                tb = k.dma(PLQ, spbc.ap()[0:16, c * T:(c + 1) * T], xdbl[64:80, 0:T], deps=[tx2])
                tc2 = k.dma(PLQ, spbc.ap()[16:32, c * T:(c + 1) * T], xdbl[80:96, 0:T], deps=[tx2])
                tdr = None
            else:
                tb = k.dma(PLQ, spbc.ap()[0:16, c * T:(c + 1) * T], xdbl[1:17, 0:T], deps=[tx2])
                tc2 = k.dma(PLQ, spbc.ap()[16:32, c * T:(c + 1) * T], xdbl[17:33, 0:T], deps=[tx2])
                tdr = k.dma(PLQ, spbc.ap()[32:33, c * T:(c + 1) * T], xdbl[0:1, 0:T], deps=[tx2])
            st["bc_sp"] = [tb, tc2] + ([tdr] if tdr else [])
            bt_toks = []
            for n in range(N):
                bt_toks.append(k.dma(PLQ, Btl[:, n * T:(n + 1) * T],
                                     spbc.ap()[n:n + 1, c * T:(c + 1) * T].partition_broadcast(128),
                                     deps=[tb, st["btl_free"]] if n == 0 else [tb]))
            d_toks = []
            if lay == 0:
                twd = k.dma(SPQ, wb[1][0:64, 0:2048], w0_dt_t, deps=[st["wb_tok"][1]])
                for dt in range(NDT):
                    bank = 5 + (dt % 2)
                    mms = [lambda p, dt=dt, bank=bank:
                           p.matmul(ps[bank][:, 0:T], wb[1][0:64, dt * 128:(dt + 1) * 128],
                                    xdbl[0:64, 0:T], start=True, stop=True)]

                    def evacd(p, dt=dt, bank=bank):
                        return p.activation(out=dT[:, dt * T:(dt + 1) * T], in_=ps[bank][:, 0:T],
                                            func=AF.Exp, bias=csb[:, 96 + dt:96 + dt + 1])
                    ct = k.mm_group(f"b{bank}", mms, "act", evacd,
                                    extra_deps=[tx2, twd, t_cc[0], st["dT_free"] if dt == 0 else None])
                    td = k.op("act", lambda p, dt=dt: p.activation(
                        out=dT[:, dt * T:(dt + 1) * T], in_=dT[:, dt * T:(dt + 1) * T],
                        func=AF.Ln, bias=1.0), deps=[ct], tok=True)
                    d_toks.append(td)
                st["wb_tok"][1] = d_toks[-1]
                st["xdbl_free"] = d_toks[-1]
                for tcp in (tb, tc2):
                    pass
            else:
                tdb = k.dma(PLQ, dtrawb[:, 0:T],
                            spbc.ap()[32:33, c * T:(c + 1) * T].partition_broadcast(128),
                            deps=[tdr, st["gcol_free"]])
                for dt in range(NDT):
                    te = k.op("act", lambda p, dt=dt: p.activation(
                        out=dT[:, dt * T:(dt + 1) * T], in_=dtrawb[:, 0:T], func=AF.Exp,
                        scale=csb[:, 112 + dt:112 + dt + 1], bias=csb[:, 96 + dt:96 + dt + 1]),
                        deps=[tdb, t_wdt1, t_cc[1], st["dT_free"] if dt == 0 else None], tok=True)
                    td = k.op("act", lambda p, dt=dt: p.activation(
                        out=dT[:, dt * T:(dt + 1) * T], in_=dT[:, dt * T:(dt + 1) * T],
                        func=AF.Ln, bias=1.0), deps=[te], tok=True)
                    d_toks.append(td)
                st["xdbl_free"] = tdb
            tspd = k.dma(PLQ, r3(sp_d.ap()[:, c * T:(c + 1) * T]),
                         s3(dT[:], 16), deps=d_toks)
            st["d_sp"][lay][c] = tspd
            du_toks = []
            for dt in range(NDT):
                du_toks.append(k.op("dve", lambda p, dt=dt: p.tensor_tensor(
                    uT[:, dt * T:(dt + 1) * T], uT[:, dt * T:(dt + 1) * T],
                    dT[:, dt * T:(dt + 1) * T], ALU.mult),
                    deps=[d_toks[dt], tspu, tx2], tok=True))
            stot_last = st.get("stot_last")
            for dt in range(NDT):
                t1 = k.op("dve", lambda p, dt=dt: p.reduce_sum(
                    lncol[:, 15:16], dT[:, dt * T:(dt + 1) * T], axis=AX.X),
                    deps=[d_toks[dt], stot_last], tok=True)
                stot_last = k.op("dve", lambda p, dt=dt: p.tensor_tensor(
                    Stot[:, dt:dt + 1], Stot[:, dt:dt + 1], lncol[:, 15:16], ALU.add),
                    deps=[t1, st["stot_z"]] if c == 0 else [t1], tok=True)
            st["stot_last"] = stot_last
            g0, g1 = emit_scan(lay, c, False, d_toks, du_toks, bt_toks)
            st["btl_free"] = g1 or g0
            st["uT_free"] = g1 or g0
            st["dT_free"] = stot_last
        dt_toks = []
        for dt in range(NDT):
            dt_toks.append(k.op("act", lambda p, dt=dt: p.activation(
                out=Dtot[:, dt * 16:(dt + 1) * 16], in_=A_sb[lay][:, dt * 16:(dt + 1) * 16],
                func=AF.Exp, scale=Stot[:, dt:dt + 1]), deps=[t_A[lay], st["dT_free"]], tok=True))
        st["stot_free"] = dt_toks[-1]
        agi, ago = ag_in[lay], ag_out[lay]
        t1 = k.dma(SPQ, agi.ap()[:, 0:256], Dtot[:], deps=dt_toks)
        t2 = k.dma(SPQ, agi.ap()[:, 256:512], carry[:],
                   deps=[st["carry_tok"][dt] for dt in range(NDT)])
        tg = k.op("pool", lambda p, agi=agi, ago=ago: p.collective_compute(
            "AllGather", ALU.bypass, ins=[agi.ap()], outs=[ago.ap()],
            replica_groups=RG), deps=[t1, t2], tok=True)
        tl = k.dma(SPQ, s3(agsb, 8), ago.ap().rearrange("(j p) c -> p j c", p=128),
                   deps=[tg, st["ygT_free"]] + list(st["xp_tail"]) + [st["xn_free"], st["xnT_free"]])
        tz = k.dma(SPQ, hin[:], c_zeros, deps=[st.get("hin_free")])
        tprev = tz
        for j in range(NC - 1):
            Dj = agsb[:, j * 512:j * 512 + 256]
            hj = agsb[:, j * 512 + 256:j * 512 + 512]
            ta = k.op("dve", lambda p, Dj=Dj: p.tensor_tensor(Dtot[:], Dj, hin[:], ALU.mult),
                      deps=[tl, tprev], tok=True)
            tb2 = k.op("dve", lambda p, hj=hj: p.tensor_tensor(Dtot[:], Dtot[:], hj, ALU.add),
                       deps=[ta], tok=True)
            tc3 = k.op("dve", lambda p: p.tensor_tensor(Dtot[:], Dtot[:], hin[:], ALU.subtract),
                       deps=[tb2], tok=True)
            tprev = k.op("dve", lambda p, j=j: p.scalar_tensor_tensor(
                hin[:], Dtot[:], chm[:, j:j + 1], hin[:], ALU.mult, ALU.add),
                deps=[tc3, t_chm], tok=True)
        st["ygT_free"] = tprev
        return tprev

    def mamba_phaseB(lay, hin_tok):
        PB_LVL = int(os.environ.get("MK_PB", "9"))
        wout = w_out[lay]
        spbc = sp_bc[lay]
        for c in range(NCH):
            last_sp = [st["u_sp"][lay][NCH - 1], st["d_sp"][lay][NCH - 1],
                       st["z_sp"][lay][NCH - 1]] if c == 0 else []
            tu = k.dma(SPQ, s3(uT[:], 16), r3(sp_u.ap()[:, c * T:(c + 1) * T]),
                       deps=[st["u_sp"][lay][c], st["uT_free"]] + last_sp)
            td = k.dma(SPQ, s3(dT[:], 16), r3(sp_d.ap()[:, c * T:(c + 1) * T]),
                       deps=[st["d_sp"][lay][c], st["dT_free"]] + last_sp)
            tz = k.dma(SPQ, s3(zT[:], 16), r3(sp_z.ap()[:, c * T:(c + 1) * T]),
                       deps=[st["z_sp"][lay][c], st["zT_free"]] + last_sp)
            st["uT_load"], st["zT_load"] = tu, tz
            if lay == 0:
                tx = k.dma(SPQ, s3(xwin[:], 2),
                           r3(x_in[3 + c * T: 3 + (c + 1) * T, :]),
                           deps=[st["xw_free"]])
            else:
                tx = k.dma(SPQ, s3(xwin[:], 2),
                           r3(xr.ap()[c * T:(c + 1) * T, :]),
                           deps=[st["xr_tok"][c], st["xw_free"]])
            bt_toks, ct_toks = [], []
            for n in range(N):
                bt_toks.append(k.dma(PLQ, Btl[:, n * T:(n + 1) * T],
                                     spbc.ap()[n:n + 1, c * T:(c + 1) * T].partition_broadcast(128),
                                     deps=[st["btl_free"]] if n == 0 else []))
                ct_toks.append(k.dma(PLQ, Ctl[:, n * T:(n + 1) * T],
                                     spbc.ap()[16 + n:17 + n, c * T:(c + 1) * T].partition_broadcast(128),
                                     deps=[st["ctl_free"]] if n == 0 else []))
            du_toks = []
            for dt in range(NDT):
                dps = [tu, td, st["du_free"]] if dt == 0 else []
                if c == 0 and dt == 0:
                    dps += [st["xn_free"], st["xnT_free"]]
                du_toks.append(k.op("dve", lambda p, dt=dt: p.tensor_tensor(
                    du_v[:, dt * T:(dt + 1) * T], uT[:, dt * T:(dt + 1) * T],
                    dT[:, dt * T:(dt + 1) * T], ALU.mult), deps=dps, tok=True))
            if PB_LVL < 2:
                st["btl_free"] = du_toks[-1]
                st["ctl_free"] = du_toks[-1]
                st["du_free"] = du_toks[-1]
                st["uT_free"] = du_toks[-1]
                st["dT_free"] = du_toks[-1]
                st["zT_free"] = tz
                st["xw_free"] = tx
                continue
            silu_toks = []
            for dt in range(NDT):
                silu_toks.append(k.op("act", lambda p, dt=dt: p.activation(
                    out=zT[:, dt * T:(dt + 1) * T], in_=zT[:, dt * T:(dt + 1) * T],
                    func=AF.Silu), deps=[tz] if dt == 0 else [], tok=(dt == NDT - 1)))
            st["zsilu"] = silu_toks[-1]
            g0, g1 = emit_scan(lay, c, True, [td] * NDT, du_toks, bt_toks,
                               ct_toks=ct_toks, hin_tok=hin_tok,
                               lite=(PB_LVL < 3))
            st["btl_free"] = g1 or g0
            st["ctl_free"] = g1 or g0
            st["du_free"] = g1 or g0
            st["uT_free"] = st["yg_tok"][NDT - 1] if PB_LVL >= 3 else (g1 or g0)
            st["dT_free"] = g1 or g0
            st["zT_free"] = st["yg_tok"][NDT - 1] if PB_LVL >= 3 else tz
            if PB_LVL < 4:
                st["xw_free"] = tx
                continue
            pe_last = None
            for dt in range(NDT):
                par = dt % 2
                tws = k.dma(SPQ, wb[par][:, 0:1024], wout[dt * 128:(dt + 1) * 128, :],
                            deps=[st["wb_tok"][par]])
                for g, (tt2, half) in enumerate(((0, 0), (0, 1), (1, 0), (1, 1))):
                    dep0 = [tws, st["yg_tok"][dt]]
                    if dt == 0:
                        dep0.append(k.bank_tok.get(f"b{g}"))
                    pe_last = k.op("pe", lambda p, dt=dt, par=par, g=g, tt2=tt2, half=half:
                                   p.matmul(ps[g][:],
                                            ygT_v[:, dt * T + tt2 * TS: dt * T + (tt2 + 1) * TS],
                                            wb[par][:, half * 512:(half + 1) * 512],
                                            start=(dt == 0), stop=(dt == NDT - 1)),
                                   deps=dep0, tok=(g == 3))
                st["wb_tok"][par] = pe_last
            ev_toks = []
            for g, (tt2, half) in enumerate(((0, 0), (0, 1), (1, 0), (1, 1))):
                te = k.op("dve", lambda p, g=g, tt2=tt2, half=half: p.tensor_tensor(
                    xwin[:, tt2 * 1024 + half * 512: tt2 * 1024 + (half + 1) * 512],
                    xwin[:, tt2 * 1024 + half * 512: tt2 * 1024 + (half + 1) * 512],
                    ps[g][:], ALU.add), deps=[pe_last, tx], tok=True)
                k.bank_tok[f"b{g}"] = te
                ev_toks.append(te)
            st["xr_tok"][c] = k.dma(PLQ, r3(xr.ap()[c * T:(c + 1) * T, :]),
                                    s3(xwin[:], 2), deps=ev_toks)
            st["xw_free"] = st["xr_tok"][c]
            st["ygT_free"] = pe_last

    def l0_ffn(tok_fold):
        tm = [k.dma(SPQ, s3(Mres[0][:], 4), r3(ag0_out.ap()[0:512, :]),
                    deps=[tok_fold, st["btl_free"]]),
              k.dma(SPQ, s3(Mres[1][:], 4), r3(ag0_out.ap()[512:1024, :]),
                    deps=[st["ctl_free"]])]
        tbia = k.dma(SPQ, f0b[:], f0_b)

        def mslice(dmt, half):
            buf = Mres[dmt // 4]
            i = dmt % 4
            return buf[:, i * 1024 + half * 512: i * 1024 + (half + 1) * 512]
        for c in range(NCH):
            tx = k.dma(SPQ, s3(xwin[:], 2),
                       r3(xr.ap()[c * T:(c + 1) * T, :]),
                       deps=[st["xr_tok"][c], st["xw_free"]])
            ttr_toks = []
            ln_toks = []
            for tt in range(2):
                t_ln = ln_tile(xwin[:, tt * 1024:(tt + 1) * 1024], xn_v, tt, 1e-5,
                               deps=[tx, st["xn_free"]])
                ln_toks.append(t_ln)
                for dmt in range(8):
                    tt_tok = transpose_tile(xn_v[:, dmt * 128:(dmt + 1) * 128],
                                            xnT_v[:, dmt * 256 + tt * 128: dmt * 256 + (tt + 1) * 128],
                                            4 + dmt % 2, [t_ln, st["xnT_free"]])
                st["xn_free"] = tt_tok
                ttr_toks.append(tt_tok)
            ev_toks = []
            for g, (tt2, half) in enumerate(((0, 0), (0, 1), (1, 0), (1, 1))):
                bank = g
                mms = []
                for dmt in range(8):
                    mms.append(lambda p, dmt=dmt, tt2=tt2, half=half, bank=bank:
                               p.matmul(ps[bank][:],
                                        xnT_v[:, dmt * 256 + tt2 * 128: dmt * 256 + (tt2 + 1) * 128],
                                        mslice(dmt, half), start=(dmt == 0), stop=False))
                mms.append(lambda p, half=half, bank=bank:
                           p.matmul(ps[bank][:], onesr[0:1, :], f0b[0:1, half * 512:(half + 1) * 512],
                                    start=False, stop=True))

                def evac(p, tt2=tt2, half=half, bank=bank):
                    return p.tensor_tensor(
                        xwin[:, tt2 * 1024 + half * 512: tt2 * 1024 + (half + 1) * 512],
                        xwin[:, tt2 * 1024 + half * 512: tt2 * 1024 + (half + 1) * 512],
                        ps[bank][:], ALU.add)
                ev_toks.append(k.mm_group(f"b{bank}", mms, "dve", evac,
                                          extra_deps=tm + [tbia, t_ones] + ttr_toks))
            st["xnT_free"] = ev_toks[-1]
            st["xr_tok"][c] = k.dma(SPQ, r3(xr.ap()[c * T:(c + 1) * T, :]),
                                    s3(xwin[:], 2), deps=ev_toks)
            st["xw_free"] = st["xr_tok"][c]
            if c == NCH - 1:
                st["ag2src"] = k.dma(SPQ, ag2_in.ap(), xwin[125:128, 1024:2048], deps=ev_toks)
                st["xw_free"] = st["ag2src"]
        st["btl_free"] = ev_toks[-1]
        st["ctl_free"] = ev_toks[-1]

    def moe():
        tgw = k.dma(SPQ, s3(gwsb[:], 8), r3(m_gw_t))
        prev_sp = None
        gate_toks = []
        for c in range(NCH):
            tx = k.dma(SPQ, s3(xw2[:], 2),
                       r3(xr.ap()[c * T:(c + 1) * T, :]),
                       deps=[st["xr_tok"][c], st.get("xw2_free"),
                             st["du_free"] if c == 0 else None, st["ygT_free"] if c == 0 else None])
            ttr_toks = []
            for tt in range(2):
                t_ln = ln_tile(xw2[:, tt * 1024:(tt + 1) * 1024], xn2, tt, 1e-5,
                               deps=[tx, st["xn2_free"] if "xn2_free" in st else None,
                                     st["d_sp"][1][NCH - 1] if c == 0 and tt == 0 else None])
                for dmt in range(8):
                    tt_tok = transpose_tile(xn2[:, dmt * 128:(dmt + 1) * 128],
                                            xnT2[:, dmt * 256 + tt * 128: dmt * 256 + (tt + 1) * 128],
                                            2 + dmt % 2, [t_ln, prev_sp])
                st["xn2_free"] = tt_tok
                ttr_toks.append(tt_tok)
            prev_sp = k.dma(SPQ, r3(sp_xnt.ap()[:, c * T:(c + 1) * T]),
                            s3(xnT2[:], 8), deps=ttr_toks)
            st["xnt_sp"] = prev_sp
            st["xn2_free"] = prev_sp
            for tt in range(2):
                bank = 6
                mms = []
                for dmt in range(8):
                    mms.append(lambda p, dmt=dmt, tt=tt, bank=bank:
                               p.matmul(ps[bank][:, 0:2],
                                        xnT2[:, dmt * 256 + tt * 128: dmt * 256 + (tt + 1) * 128],
                                        gwsb[:, dmt * 2:(dmt + 1) * 2], start=(dmt == 0), stop=(dmt == 7)))

                def evg(p, bank=bank):
                    return p.copy(tcol[:, 0:2], ps[bank][:, 0:2])
                tge = k.mm_group(f"b{bank}", mms, "act", evg,
                                 extra_deps=ttr_toks + [tgw] +
                                 ([gate_toks[-1]] if gate_toks else [st["gcol_free"]]))
                gate_toks.append(k.op("dve", lambda p, c=c, tt=tt: p.tensor_tensor(
                    m0buf[:, c * 2 + tt:c * 2 + tt + 1],
                    tcol[:, 0:1], tcol[:, 1:2], ALU.is_ge), deps=[tge], tok=True))
        bank = 7
        mms = [lambda p: p.transpose(ps[bank][0:16, 0:128], m0buf[:], eye[:])]

        def evm(p):
            return p.copy(hxs[0:16, 0:128], ps[bank][0:16, 0:128])
        tev = k.mm_group(f"b{bank}", mms, "act", evm, extra_deps=gate_toks + [t_eye])
        tw0s = []
        for r16 in range(16):
            tw0s.append(k.dma(SPQ, sp_m0.ap()[0:1, r16 * 128:(r16 + 1) * 128],
                              hxs[r16:r16 + 1, 0:128], deps=[tev] if r16 == 0 else []))
        tmb = k.dma(PLQ, m0til, sp_m0.ap().partition_broadcast(128),
                    deps=tw0s + [st["du_free"]])
        m1til = pv(43008, 2048)
        tmb1 = k.op("dve", lambda p: p.tensor_scalar(
            m1til, m0til, cst[:, 5:6], cst[:, 6:7], ALU.mult, ALU.add),
            deps=[tmb, t_cst, st["wb_tok"][0], st["wb_tok"][1]], tok=True)
        tk = [k.dma(SPQ, kbuf[:, 0:32], m_b1[0][0].rearrange("(a p) -> p a", p=128), slow=True),
              k.dma(SPQ, kbuf[:, 32:64], m_b1[1][0].rearrange("(a p) -> p a", p=128), slow=True),
              k.dma(SPQ, kbuf[:, 64:96], m_b1[2][0].rearrange("(a p) -> p a", p=128), slow=True)]
        tb1c = k.op("dve", lambda p: p.tensor_copy(b1col[:], kbuf[:]), deps=tk, tok=True)
        tkr = k.op("act", lambda p: p.activation(out=kbuf[:, 0:64], in_=kbuf[:, 0:64], func=AF.Relu),
                   deps=[tb1c], tok=True)

        w1_tok = [None, None, None]
        w2_tok = [None, None, None]
        ksb_tok = None
        xnw_tok = None
        hb_free = [None, None, None]
        last_evh = [None, None, None]
        yacc_last = None
        for hb in range(NHB):
            for f in range(3):
                g0dep = ([k.scan_guard.get(0), k.scan_guard.get(1)] if hb == 0 else
                         [last_evh[f], yacc_last, ksb_tok])
                w1_tok[f] = k.dma(SPQ, s3(w1s[f][:], 8),
                                  r3(m_w1[f][:, hb * HB:(hb + 1) * HB]),
                                  deps=[w1_tok[f]] + g0dep)
                w2_tok[f] = k.dma(SPQ, s3(w2s[f][:], 2),
                                  r3(m_w2[f][hb * HB:(hb + 1) * HB, :]),
                                  deps=[w2_tok[f]] + g0dep)
            for e in range(2):
                for half in range(2):
                    bank = 6
                    mms = []
                    for ht in range(2):
                        mms.append(lambda p, e=e, ht=ht, half=half, hb=hb, bank=bank:
                                   p.matmul(ps[bank][0:1, 0:512],
                                            kbuf[:, e * 32 + hb * 2 + ht: e * 32 + hb * 2 + ht + 1],
                                            w2s[e][:, ht * 1024 + half * 512: ht * 1024 + (half + 1) * 512],
                                            start=(ht == 0), stop=(ht == 1)))

                    def evk(p, e=e, half=half, hb=hb, bank=bank):
                        dst = ksb[0:1, e * 1024 + half * 512:e * 1024 + (half + 1) * 512]
                        if hb == 0:
                            return p.tensor_copy(dst, ps[bank][0:1, 0:512])
                        return p.tensor_tensor(dst, dst, ps[bank][0:1, 0:512], ALU.add)
                    ksb_tok = k.mm_group(f"b{bank}", mms, "dve", evk,
                                         extra_deps=[tkr, w2_tok[e], ksb_tok] +
                                                    ([gate_toks[-1]] if hb == 0 else []))
            for tc in range(4):
                txn = k.dma(SPQ, s3(xnw, 8),
                            r3(sp_xnt.ap()[:, tc * 512:(tc + 1) * 512]),
                            deps=[st["xnt_sp"], xnw_tok] +
                                 ([st["ygT_free"], st["du_free"], k.scan_guard.get(0),
                                   k.scan_guard.get(1)] if hb == 0 and tc == 0 else []))
                h_toks = []
                for f in range(3):
                    for ht in range(2):
                        bank = ht
                        mms = []
                        for dmt in range(8):
                            mms.append(lambda p, f=f, ht=ht, dmt=dmt, bank=bank:
                                       p.matmul(ps[bank][:],
                                                w1s[f][:, dmt * HB + ht * 128: dmt * HB + (ht + 1) * 128],
                                                xnw[:, dmt * 512:(dmt + 1) * 512],
                                                start=(dmt == 0), stop=(dmt == 7)))

                        def evh(p, f=f, ht=ht, hb=hb, bank=bank):
                            return p.activation(out=hbv[f][:, ht * 512:(ht + 1) * 512], in_=ps[bank][:],
                                                func=AF.Relu,
                                                bias=b1col[:, f * 32 + hb * 2 + ht: f * 32 + hb * 2 + ht + 1])
                        ct = k.mm_group(f"b{bank}", mms, "act", evh,
                                        extra_deps=[txn, w1_tok[f], tb1c, hb_free[f]] +
                                        ([st["btl_free"], st["ctl_free"]] if hb == 0 and tc == 0 else []))
                        last_evh[f] = ct
                        h_toks.append(ct)
                mk_toks = []
                for f in range(2):
                    mt = m0til if f == 0 else m1til
                    for ht in range(2):
                        mk_toks.append(k.op("dve", lambda p, f=f, ht=ht, tc=tc, mt=mt: p.tensor_tensor(
                            hbv[f][:, ht * 512:(ht + 1) * 512], hbv[f][:, ht * 512:(ht + 1) * 512],
                            mt[:, tc * 512:(tc + 1) * 512], ALU.mult),
                            deps=[h_toks[f * 2 + ht], tmb, tmb1], tok=True))
                hdeps = mk_toks + h_toks[4:6]
                for dmt in range(8):
                    bank = 2 + (dmt % 2)
                    mms = []
                    for f in range(3):
                        for ht in range(2):
                            mms.append(lambda p, f=f, ht=ht, dmt=dmt, bank=bank:
                                       p.matmul(ps[bank][:],
                                                w2s[f][:, ht * 1024 + dmt * 128: ht * 1024 + dmt * 128 + 128],
                                                hbv[f][:, ht * 512:(ht + 1) * 512],
                                                start=(f == 0 and ht == 0), stop=(f == 2 and ht == 1)))

                    def evy(p, dmt=dmt, tc=tc, hb=hb, bank=bank):
                        dst = y_acc[:, dmt * 2048 + tc * 512: dmt * 2048 + (tc + 1) * 512]
                        if hb == 0:
                            return p.tensor_copy(dst, ps[bank][:])
                        return p.tensor_tensor(dst, dst, ps[bank][:], ALU.add)
                    yacc_last = k.mm_group(f"b{bank}", mms, "dve", evy,
                                           extra_deps=hdeps + [w2_tok[0], w2_tok[1], w2_tok[2]] +
                                           ([st["xw_free"], st["uT_free"], st["zT_free"],
                                             st["btl_free"], st["dT_free"]] if hb == 0 else []))
                hb_free[0] = hb_free[1] = hb_free[2] = yacc_last
                xnw_tok = yacc_last
        gdep = [gate_toks[-1], prev_sp]
        tb2 = [k.dma(SPQ, crow[0:1, 0:1024], m_b2e0, deps=gdep),
               k.dma(SPQ, crow[0:1, 1024:2048], m_b2e1, deps=gdep),
               k.dma(SPQ, sb2row, m_sb2, deps=gdep)]
        tc1 = k.op("dve", lambda p: p.tensor_tensor(crow[0:1, 0:1024], crow[0:1, 0:1024],
                                                    crow[0:1, 1024:2048], ALU.add), deps=tb2, tok=True)
        tc2 = k.op("dve", lambda p: p.tensor_tensor(crow[0:1, 0:1024], crow[0:1, 0:1024], sb2row,
                                                    ALU.add), deps=[tc1], tok=True)
        tc3 = k.op("dve", lambda p: p.tensor_tensor(crow[0:1, 0:1024], crow[0:1, 0:1024],
                                                    ksb[0:1, 0:1024], ALU.add),
                   deps=[tc2, ksb_tok], tok=True)
        tc4 = k.op("dve", lambda p: p.tensor_tensor(crow[0:1, 1024:2048], ksb[0:1, 1024:2048],
                                                    ksb[0:1, 0:1024], ALU.subtract),
                   deps=[tc3], tok=True)
        tw1 = k.dma(SPQ, sp_k.ap(), crow, deps=[tc4])
        tw2a = k.dma(SPQ, kcols[:, 0:8],
                     sp_k.ap()[0][0:1024].rearrange("(a p) -> p a", p=128), deps=[tw1], slow=True)
        tw2 = k.dma(SPQ, kcols[:, 8:16],
                     sp_k.ap()[0][1024:2048].rearrange("(a p) -> p a", p=128), deps=[tw2a], slow=True)
        kc_toks = []
        for dmt in range(8):
            t1 = k.op("dve", lambda p, dmt=dmt: p.scalar_tensor_tensor(
                y_acc[:, dmt * 2048:(dmt + 1) * 2048], m0til, kcols[:, 8 + dmt:8 + dmt + 1],
                y_acc[:, dmt * 2048:(dmt + 1) * 2048], ALU.mult, ALU.add),
                deps=[tw2, yacc_last, tmb], tok=True)
            kc_toks.append(k.op("act", lambda p, dmt=dmt: p.add(
                y_acc[:, dmt * 2048:(dmt + 1) * 2048], y_acc[:, dmt * 2048:(dmt + 1) * 2048],
                kcols[:, dmt:dmt + 1]), deps=[t1], tok=True))
        out_dma = []
        xw2_free = None
        for c in range(NCH):
            tx = k.dma(SPQ, s3(xw2[:], 2),
                       r3(xr.ap()[c * T:(c + 1) * T, :]),
                       deps=[xw2_free] if xw2_free else [tw2])
            adds = []
            for dmt in range(8):
                for tt in range(2):
                    bank = 4 + ((dmt * 2 + tt) % 2)
                    mms = [lambda p, dmt=dmt, tt=tt, c=c, bank=bank:
                           p.transpose(ps[bank][:, 0:128],
                                       y_acc[:, dmt * 2048 + c * T + tt * 128: dmt * 2048 + c * T + (tt + 1) * 128],
                                       eye[:])]

                    def evt(p, dmt=dmt, tt=tt, bank=bank):
                        return p.tensor_tensor(
                            xw2[:, tt * 1024 + dmt * 128: tt * 1024 + (dmt + 1) * 128],
                            xw2[:, tt * 1024 + dmt * 128: tt * 1024 + (dmt + 1) * 128],
                            ps[bank][:, 0:128], ALU.add)
                    adds.append(k.mm_group(f"b{bank}", mms, "dve", evt,
                                           extra_deps=kc_toks + [tx, t_eye]))
            tout = k.dma(SPQ, r3(out_t[c * T:(c + 1) * T, :]),
                         s3(xw2[:], 2), deps=adds)
            out_dma.append(tout)
            xw2_free = tout
        def final_wait(p, toks=tuple(out_dma)):
            for sem, val in toks:
                p.wait_ge(k.h[sem], val)
            return p.engine_nop() if False else p.nop()
        k.op("pool", lambda p: p.nop(), deps=out_dma)

    STAGE = int(os.environ.get("MK_STAGE", "99"))
    tok_fold = emit_fold()
    for kk in ("zT_free", "uT_free", "dT_free", "btl_free", "ctl_free", "xn_free",
               "xnT_free", "xw_free"):
        st[kk] = tok_fold
    k.scan_guard[0] = tok_fold
    k.scan_guard[1] = tok_fold
    if STAGE >= 1:
        hin0 = mamba_phaseA(0)
    if STAGE >= 2:
        mamba_phaseB(0, hin0)
        st["hin_free"] = st["yg_tok"][NDT - 1]
    if STAGE >= 3:
        l0_ffn(tok_fold)
    if STAGE >= 4:
        hin1 = mamba_phaseA(1)
    if STAGE >= 5:
        mamba_phaseB(1, hin1)
    if STAGE >= 6:
        moe()
    if STAGE < 6:
        touts = []
        for c in range(NCH):
            touts.append(k.dma(SPQ, out_t[c * T:(c + 1) * T, :], xr.ap()[c * T:(c + 1) * T, :],
                               deps=[st["xr_tok"][c]] if st["xr_tok"][c] else []))
        k.op("pool", lambda p: p.nop(), deps=touts)

    import contextlib
    with contextlib.ExitStack() as stack:
        for sn in k.semnames:
            k.h[sn] = stack.enter_context(nc.semaphore("s_" + sn))
        block = stack.enter_context(nc.Block())

        @block.tensor
        def _(p):
            for fn in k.q["pe"]:
                fn(p)

        @block.scalar
        def _(p):
            for fn in k.q["act"]:
                fn(p)

        @block.vector
        def _(p):
            for fn in k.q["dve"]:
                fn(p)

        @block.gpsimd
        def _(p):
            for fn in k.q["pool"]:
                fn(p)

        @block.sync
        def _(p):
            for fn in k.q["sp"]:
                fn(p)

    return nc


_NC_CACHE = [None]


def kernel(**inputs):
    f32 = np.float32

    def c(a):
        return np.ascontiguousarray(a, dtype=f32)

    x = c(inputs["L0_attn_x"]).reshape(16384, 1024)
    D = 1024
    jj = np.arange(D)
    ang = 2.0 * np.pi / D * np.outer(jj, jj)
    Fr = np.cos(ang).astype(f32)
    Fi = (-np.sin(ang)).astype(f32)
    Grd = (np.cos(ang) / D).astype(f32)
    Gnid = (-np.sin(ang) / D).astype(f32)
    eye = np.eye(128, dtype=f32)

    base = {
        "w0_in_t": c(inputs["L0_attn_W_in"].T),
        "w0_conv": c(inputs["L0_attn_conv_w"].reshape(2048, 4)),
        "w0_convb": c(inputs["L0_attn_conv_b"].reshape(2048, 1)),
        "w0_x_t": c(inputs["L0_attn_W_x"].T),
        "w0_dt_t": c(inputs["L0_attn_W_dt"].T),
        "w0_dtb": c(inputs["L0_attn_b_dt"].reshape(2048, 1)),
        "w0_alog": c(inputs["L0_attn_A_log"]),
        "w0_dp": c(inputs["L0_attn_D_param"].reshape(2048, 1)),
        "w0_out": c(inputs["L0_attn_W_out"].T),
        "f0_wr": c(inputs["L0_ffn_w_freq_real"]),
        "f0_wi": c(inputs["L0_ffn_w_freq_imag"]),
        "f0_b": c(inputs["L0_ffn_bias"].reshape(1, 1024)),
        "w1_normw": c(inputs["L1_attn_norm_w"].reshape(1, 1024)),
        "w1_in": c(inputs["L1_attn_W_in"]),
        "w1_conv": c(inputs["L1_attn_conv_w"].reshape(2048, 4)),
        "w1_convb": c(inputs["L1_attn_conv_b"].reshape(2048, 1)),
        "w1_x": c(inputs["L1_attn_W_x"]),
        "w1_wdt": c(inputs["L1_attn_w_dt"].reshape(2048, 1)),
        "w1_dtb": c(inputs["L1_attn_b_dt"].reshape(2048, 1)),
        "w1_alog": c(inputs["L1_attn_A_log"]),
        "w1_dp": c(inputs["L1_attn_D_param"].reshape(2048, 1)),
        "w1_out": c(inputs["L1_attn_W_out"]),
        "m_gw_t": c(inputs["L1_ffn_gate_w"].T),
        "m_w1e0": c(inputs["L1_ffn_w1"][0]),
        "m_w1e1": c(inputs["L1_ffn_w1"][1]),
        "m_sw1": c(inputs["L1_ffn_sw1"]),
        "m_b1e0": c(inputs["L1_ffn_b1"][0].reshape(1, 4096)),
        "m_b1e1": c(inputs["L1_ffn_b1"][1].reshape(1, 4096)),
        "m_sb1": c(inputs["L1_ffn_sb1"].reshape(1, 4096)),
        "m_w2e0": c(inputs["L1_ffn_w2"][0]),
        "m_w2e1": c(inputs["L1_ffn_w2"][1]),
        "m_sw2": c(inputs["L1_ffn_sw2"]),
        "m_b2e0": c(inputs["L1_ffn_b2"][0].reshape(1, 1024)),
        "m_b2e1": c(inputs["L1_ffn_b2"][1].reshape(1, 1024)),
        "m_sb2": c(inputs["L1_ffn_sb2"].reshape(1, 1024)),
        "c_eye": eye,
        "c_ones": np.ones((1, 128), f32),
        "c_zeros": np.zeros((128, 256), f32),
        "c_cst": np.tile(np.array([1e-5, 1e-6, -0.5, 1.0 / 1024.0, 0.0, -1.0, 1.0, 0.0], f32),
                         (128, 1)),
        "c_grd": Grd,
        "c_gnid": Gnid,
    }

    if _NC_CACHE[0] is None:
        _NC_CACHE[0] = build_program()
    nc = _NC_CACHE[0]

    in_maps = []
    for i in range(NC):
        d = dict(base)
        xs = np.zeros((L + 3, DM), f32)
        xs[3:] = x[i * L:(i + 1) * L]
        if i > 0:
            xs[:3] = x[i * L - 3:i * L]
        d["x"] = xs
        d["c_frs"] = c(Fr[:, i * 128:(i + 1) * 128])
        d["c_fips"] = c(Fi[:, i * 128:(i + 1) * 128])
        d["c_fins"] = c(-Fi[:, i * 128:(i + 1) * 128])
        chm = np.zeros((128, 8), f32)
        chm[:, :i] = 1.0
        d["c_chm"] = chm
        sel3 = np.zeros((24, 3), f32)
        if i > 0:
            for r in range(3):
                sel3[3 * (i - 1) + r, r] = 1.0
        d["c_sel3"] = sel3
        in_maps.append(d)

    res = bass_utils.run_bass_kernel_spmd(nc, in_maps, core_ids=list(range(NC)))
    out = np.concatenate([np.asarray(res.results[i]["out"]) for i in range(NC)], axis=0)
    return out.reshape(1, 16384, 1024).astype(np.float32)


if __name__ == "__main__":
    build_program()
    print("build ok")
